# version 12
# baseline (speedup 1.0000x reference)
import numpy as np
import ml_dtypes
from contextlib import ExitStack

import concourse.bass as bass
import concourse.mybir as mybir
from concourse.bass_utils import run_bass_kernel_spmd

F32 = mybir.dt.float32
BF16 = mybir.dt.bfloat16
AF = mybir.ActivationFunctionType
ALU = mybir.AluOpType
AX = mybir.AxisListType

ENGINES = ("tensor", "vector", "scalar", "gpsimd", "sync")

N = 4096
D = 1024
E = 8
H = 4096
NDT = D // 128
NHB = H // 128
CH = 512
NCH = N // CH
NT_CH = CH // 128
NNT = N // 128
HEADS = 16
PAIRS = 8
EPS = 1e-5
STAB = 1e-6

DEBUG_TAPS = ()
UPTO = "full"


class Prog:
    def __init__(self):
        self.ops = {e: [] for e in ENGINES}
        self.cnt = {}

    def sem(self, name):
        self.cnt.setdefault(name, 0)
        return name

    def add(self, eng, emit, waits=(), inc=None, tag=""):
        waits = [(s, v) for (s, v) in waits if v > 0]
        for s, v in waits:
            assert s in self.cnt, f"[{tag}] wait on unknown sem {s}"
            assert v <= self.cnt[s], (
                f"[{tag}] wait {s}>={v} references future (planned {self.cnt[s]})")
        post = None
        if inc is not None:
            s, amt = inc
            self.cnt.setdefault(s, 0)
            self.cnt[s] += amt
            post = self.cnt[s]
        self.ops[eng].append((emit, waits, inc, tag))
        return post

    def verify(self):
        idx = {e: 0 for e in ENGINES}
        val = {s: 0 for s in self.cnt}
        progress = True
        while progress:
            progress = False
            for e in ENGINES:
                while idx[e] < len(self.ops[e]):
                    emit, waits, inc, tag = self.ops[e][idx[e]]
                    if any(val[s] < v for s, v in waits):
                        break
                    if inc is not None:
                        val[inc[0]] += inc[1]
                    idx[e] += 1
                    progress = True
        stuck = {e: idx[e] for e in ENGINES if idx[e] < len(self.ops[e])}
        if stuck:
            lines = []
            for e, i in stuck.items():
                emit, waits, inc, tag = self.ops[e][i]
                missing = [(s, v, val[s]) for s, v in waits if val[s] < v]
                lines.append(f"  {e}[{i}] tag={tag} missing={missing}")
            raise RuntimeError("DEADLOCK in schedule:\n" + "\n".join(lines))
        for s, v in self.cnt.items():
            assert val[s] == v, f"sem {s} ended {val[s]} != planned {v}"

    def emit(self, nc, ctx):
        handles = {s: ctx.enter_context(nc.semaphore(s)) for s in self.cnt}
        block = ctx.enter_context(nc.Block())

        def runner(ename):
            def run(eng):
                for emit, waits, inc, tag in self.ops[ename]:
                    for s, v in waits:
                        eng.wait_ge(handles[s], v)
                    inst = emit(eng)
                    if inc is not None:
                        assert inst is not None, f"inc w/o instruction: {tag}"
                        inst.then_inc(handles[inc[0]], inc[1])
            return run

        block.tensor(runner("tensor"))
        block.vector(runner("vector"))
        block.scalar(runner("scalar"))
        block.gpsimd(runner("gpsimd"))
        block.sync(runner("sync"))


def build_program(debug_taps=(), upto="full"):
    nc = bass.Bass("TRN2", target_bir_lowering=False, debug=False)
    p = Prog()
    for s in ("gp_c", "s1", "s2", "gelu", "evac", "w2f", "corr", "ev1",
              "ln_act", "ln_dve", "ln_mm", "gate_mm", "gate_dve",
              "at_mm", "at_act", "at_dve"):
        p.sem(s)
    DMA_SEMS = (["d_cst_s", "d_cst_g", "d_rf", "d_wqkv", "d_wout"]
                + [f"d_w1q{i}" for i in range(8)]
                + [f"d_w2q{i}" for i in range(8)]
                + ["d_b2_0", "d_b2_1"]
                + [f"d_xst{i}" for i in range(4)]
                + ["d_utw0", "d_utw1", "d_utw2", "d_utw3",
                   "d_utb0", "d_utb1",
                   "d_xin0", "d_xin1", "d_xout0", "d_xout1",
                   "d_atxi0", "d_atxi1", "d_atxo0", "d_atxo1"])
    for s in DMA_SEMS:
        p.sem(s)
    K = {}

    def din(name, shape, dtype=F32):
        return nc.dram_tensor(name, list(shape), dtype, kind="ExternalInput").ap()

    x_in = din("x", (N, D))
    wqkv_in = din("wqkv", (D, 3 * D))
    rf_in = din("rf", (HEADS, 64, 64))
    wout_in = din("wout", (D, D))
    bout_in = din("bout", (1, D))
    wg_in = din("wg", (D, E))
    bg_in = din("bg", (1, E))
    w1_in = din("w1", (E, D, H))
    b1t_in = din("b1t", (E, 128, NHB))
    w2_in = din("w2", (E, H, D))
    b2_in = din("b2", (E, D))
    ident_in = din("ident", (128, 128), BF16)
    bcmask_in = din("bcmask", (2, 128), BF16)
    qmask_in = din("qmask", (128, 16), BF16)

    y = nc.dram_tensor("y", [N, D], F32, kind="ExternalOutput").ap()
    xbuf_kind = ("ExternalOutput"
                 if ("xbuf_ffn1" in debug_taps or "xbuf_attn" in debug_taps)
                 else "Internal")
    xbuf = nc.dram_tensor("xbuf", [N, D], F32, kind=xbuf_kind).ap()
    uT_kind = "ExternalOutput" if "uT" in debug_taps else "Internal"
    uT_hbm = nc.dram_tensor("uT_hbm", [128, NNT * 1024], BF16, kind=uT_kind).ap()
    dbg_sel = (nc.dram_tensor("dbg_sel", [128, NNT * E], F32,
                              kind="ExternalOutput").ap()
               if "sel" in debug_taps else None)

    ARENA = 212352

    cur = [0]

    def nxt(nbytes, align=32):
        cur[0] = (cur[0] + align - 1) // align * align
        off = cur[0]
        cur[0] += nbytes
        return off

    PO_IDENT = nxt(256)
    PO_ONESR = nxt(256)
    PO_ONESC = nxt(32)
    PO_BCMASK = nxt(256)
    PO_RF = nxt(2048)
    PO_WG = nxt(128)
    PO_BGROW = nxt(32)
    PO_B1COL = nxt(1024)
    PO_SEL = nxt(1024)
    PO_BOUTR = nxt(2048)
    PO_KSBD = nxt(32)
    PO_KVBD = nxt(2048)
    PO_QM = nxt(32)
    PO_SELBF = nxt(512)
    PO_SELT = nxt(1024)
    PO_B2R = nxt(2048)
    BIG = nxt(0, 64)
    assert BIG <= 16 * 1024, BIG

    FO_W1 = BIG
    FO_W2 = FO_W1 + 65536
    FO_HT = FO_W2 + 65536
    FO_UT = FO_HT + 32768
    FO_XIN = FO_UT + 16384
    FO_XOUT = FO_XIN + 8192
    FO_END = FO_XOUT + 8192
    assert FO_END <= ARENA, FO_END

    LO = FO_HT
    LO_XST = LO
    LO_SQB = LO_XST + 16384
    LO_UBF = LO_SQB + 4096
    LO_UTS = LO_UBF + 8192
    LO_GT = LO_UTS + 8192
    LO_GS = LO_GT + 4096
    LO_STAT = LO_GS + 384
    LO_END = LO_STAT + 768
    assert LO_END <= FO_XIN

    AO_UT = BIG
    AO_WQKV = AO_UT + 65536
    AO_WOUT = AO_WQKV + 49152
    AO_P1 = AO_WOUT + 16384
    AO_KT = (LO_END + FO_HT - LO + 31) // 32 * 32
    AO_V = AO_KT + 8192
    AO_PHIK = AO_V + 8192
    assert AO_PHIK + 8192 <= ARENA, AO_PHIK
    AO_QT = AO_P1
    AO_PHIQ = AO_QT + 8192
    AO_ATTNT = AO_PHIQ + 8192
    AO_RB = AO_ATTNT + 8192
    AO_XIN = AO_RB + 2048
    AO_XOUT = AO_XIN + 8192
    AO_NRM = AO_XOUT + 8192
    AO_NRM2 = AO_NRM + 2048
    AO_REC = AO_NRM2 + 2048
    AO_REC2 = AO_REC + 2048
    AO_RECBF = AO_REC2 + 2048
    AO_RECBF2 = AO_RECBF + 1024
    AO_RB2 = AO_RECBF2 + 1024
    assert AO_RB2 + 2048 <= AO_PHIK + 8192
    AO_KVACC = LO_GT
    assert AO_KVACC + 4224 <= LO_STAT, (AO_KVACC, LO_STAT)
    assert AO_PHIK + 8192 <= ARENA, AO_PHIK

    with ExitStack() as ctx:
        arena = ctx.enter_context(
            nc.sbuf_tensor("arena", [128, ARENA], mybir.dt.uint8))

        def carve(off, part, cols, dtype):
            nb = cols * mybir.dt.size(dtype)
            assert off + nb <= ARENA
            return arena[0:part, off:off + nb].bitcast(dtype)

        pt = [ctx.enter_context(nc.psum_tensor(f"pt{i}", [128, 512], F32))
              for i in range(8)]

        ident = carve(PO_IDENT, 128, 128, BF16)
        ones_row = carve(PO_ONESR, 1, 128, BF16)
        ones_col = carve(PO_ONESC, 128, 1, BF16)
        bcmask = carve(PO_BCMASK, 2, 128, BF16)
        rf_sb = carve(PO_RF, 128, 1024, BF16)
        wg_sb = carve(PO_WG, 128, 64, BF16)
        bg_row = carve(PO_BGROW, 1, 8, BF16)
        b1col = carve(PO_B1COL, 128, 256, F32)
        sel_sb = carve(PO_SEL, 128, 256, F32)
        bout_row = carve(PO_BOUTR, 1, 1024, BF16)
        ksum_bd = carve(PO_KSBD, 128, 16, BF16)
        kv_bd = carve(PO_KVBD, 128, 1024, BF16)
        ksum_qm = carve(PO_QM, 128, 16, BF16)
        selbf = carve(PO_SELBF, 128, 256, BF16)
        selT = carve(PO_SELT, 8, 512, BF16)
        b2r = carve(PO_B2R, 8, 1024, BF16)

        w1sb = carve(FO_W1, 128, 32 * 1024, BF16)
        w2sb = carve(FO_W2, 128, 32 * 1024, BF16)
        hT = carve(FO_HT, 128, 32 * 512, BF16)
        uTb = [carve(FO_UT + i * 8192, 128, 8 * 512, BF16) for i in range(2)]
        xinb = [carve(FO_XIN + i * 4096, 128, 1024, F32) for i in range(2)]
        xoutb = [carve(FO_XOUT + i * 4096, 128, 1024, F32) for i in range(2)]

        xst = [carve(LO_XST + i * 4096, 128, 1024, F32) for i in range(4)]
        ubf = [carve(LO_UBF + i * 2048, 128, 1024, BF16) for i in range(4)]
        uts = carve(LO_UTS, 128, 4096, BF16)
        gt = [carve(LO_GT + i * 1024, 128, 256, F32) for i in range(4)]
        gs = carve(LO_GS, 128, 96, F32)
        stat = carve(LO_STAT, 128, 192, F32)

        at_uT = carve(AO_UT, 128, 8 * 4096, BF16)
        at_wqkv = carve(AO_WQKV, 128, 8 * 3072, BF16)
        at_wout = carve(AO_WOUT, 128, 8 * 1024, BF16)
        at_kT = carve(AO_KT, 128, 8 * 512, BF16)
        at_v = carve(AO_V, 128, 4 * 1024, BF16)
        at_phik = carve(AO_PHIK, 128, 4 * 1024, BF16)
        at_qT = carve(AO_QT, 128, 8 * 512, BF16)
        at_phiq = carve(AO_PHIQ, 128, 8 * 512, BF16)
        at_attnT = carve(AO_ATTNT, 128, 8 * 512, BF16)
        at_rb = carve(AO_RB, 128, 512, F32)
        at_xin = [carve(AO_XIN + i * 4096, 128, 1024, F32) for i in range(2)]
        at_xout = [carve(AO_XOUT + i * 4096, 128, 1024, F32) for i in range(2)]
        nrm_sb = [carve(AO_NRM, 2, 512, F32), carve(AO_NRM2, 2, 512, F32)]
        rec_sb = [carve(AO_REC, 2, 512, F32), carve(AO_REC2, 2, 512, F32)]
        rec_bf = [carve(AO_RECBF, 2, 512, BF16),
                  carve(AO_RECBF2, 2, 512, BF16)]
        kv_acc = carve(AO_KVACC, 128, 8 * 132, F32)
        at_rb2 = carve(AO_RB2, 128, 512, F32)

        p.add("sync", lambda e: e.dma_start(out=ident, in_=ident_in),
              inc=("d_cst_s", 16), tag="ld_ident")
        K["ident"] = ("d_cst_s", p.cnt["d_cst_s"])
        p.add("sync", lambda e: e.dma_start(out=bcmask, in_=bcmask_in),
              waits=[("d_cst_s", 16)], inc=("d_cst_s", 16), tag="ld_bcmask")
        K["bcmask"] = ("d_cst_s", p.cnt["d_cst_s"])
        p.add("sync", lambda e: e.dma_start(out=ksum_qm, in_=qmask_in),
              waits=[("d_cst_s", 32)], inc=("d_cst_s", 16), tag="ld_qmask")
        K["qmask"] = ("d_cst_s", p.cnt["d_cst_s"])

        p.add("gpsimd", lambda e: e.memset(ones_row, 1.0), inc=("gp_c", 1),
              tag="ones_row")
        p.add("gpsimd", lambda e: e.memset(ones_col, 1.0), inc=("gp_c", 1),
              tag="ones_col")
        p.add("gpsimd", lambda e: e.memset(rf_sb, 0.0), inc=("gp_c", 1),
              tag="rf_zero")
        p.add("gpsimd", lambda e: e.memset(kv_bd, 0.0), inc=("gp_c", 1),
              tag="kvbd_zero")
        p.add("gpsimd", lambda e: e.memset(ksum_bd, 0.0), inc=("gp_c", 1),
              tag="ksbd_zero")
        K["ones"] = p.cnt["gp_c"]
        for h in range(HEADS):
            pr, q = h // 2, h % 2
            dst = rf_sb[64 * q:64 * (q + 1),
                        pr * 128 + 64 * q:pr * 128 + 64 * (q + 1)]
            p.add("gpsimd", lambda e, d=dst, s=rf_in[h]: e.dma_start(out=d, in_=s),
                  waits=[("gp_c", K["ones"]), ("d_rf", 16 * h)],
                  inc=("d_rf", 16), tag=f"rf{h}")
        K["rf"] = ("d_rf", p.cnt["d_rf"])
        p.add("gpsimd", lambda e: e.dma_start(
            out=wg_sb.rearrange("p (dt k) -> p dt k", k=E),
            in_=wg_in.rearrange("(dt p) k -> p dt k", p=128)),
            inc=("d_cst_g", 16), tag="wg")
        K["wg"] = ("d_cst_g", p.cnt["d_cst_g"])
        p.add("gpsimd", lambda e: e.dma_start(out=bg_row, in_=bg_in),
              waits=[("d_cst_g", 16)], inc=("d_cst_g", 16), tag="bg")
        K["bg"] = ("d_cst_g", p.cnt["d_cst_g"])
        p.add("gpsimd", lambda e: e.dma_start(out=bout_row, in_=bout_in),
              waits=[("d_cst_g", 32)], inc=("d_cst_g", 16), tag="bout")
        K["bout"] = ("d_cst_g", p.cnt["d_cst_g"])
        p.add("gpsimd", lambda e: e.dma_start(
            out=b1col.rearrange("p (k hb) -> p k hb", k=E),
            in_=b1t_in.rearrange("k p hb -> p k hb")),
            waits=[("d_cst_g", 48)], inc=("d_cst_g", 16), tag="b1col")
        K["b1col"] = ("d_cst_g", p.cnt["d_cst_g"])
        p.add("gpsimd", lambda e: e.dma_start(out=b2r, in_=b2_in),
              waits=[("d_cst_g", 64)], inc=("d_cst_g", 16), tag="b2r")
        K["b2r"] = ("d_cst_g", p.cnt["d_cst_g"])

        def load_w1q(call, e_, q, waits, tag):
            dst = w1sb[:, q * 4096:(q + 1) * 4096].rearrange(
                "p (dt k) -> p dt k", dt=NDT)
            src = w1_in[e_].rearrange("(dt p) hh -> p dt hh",
                                      p=128)[:, :, q * 512:(q + 1) * 512]
            sem = f"d_w1q{q}"
            cnt = p.add(
                "gpsimd", lambda e, d=dst, s=src: e.dma_start(out=d, in_=s),
                waits=waits, inc=(sem, 16), tag=tag)
            K[("w1", call, e_, q)] = (sem, cnt)

        def load_w2q(call, e_, q, waits, tag):
            dst = w2sb[:, q * 4096:(q + 1) * 4096].rearrange(
                "p (hb f) -> p hb f", hb=4)
            src = w2_in[e_, q * 512:(q + 1) * 512, :].rearrange(
                "(hb pp) f -> pp hb f", pp=128)
            sem = f"d_w2q{q}"
            cnt = p.add(
                "gpsimd", lambda e, d=dst, s=src: e.dma_start(out=d, in_=s),
                waits=waits, inc=(sem, 16), tag=tag)
            K[("w2", call, e_, q)] = (sem, cnt)

        ln_idx = [0]

        def emit_ln(src_hbm, gate, ut_dst, xwait_fn=None, phase="",
                    chunk_cb=None):
            ph = ln_idx[0]
            ln_idx[0] += 1
            norm_done, evac_done, tp_done = {}, {}, {}
            bn_done, pack_done, aggr_done, recip_done, xdma = {}, {}, {}, {}, {}
            gmm_done, sqrt_done = {}, {}
            utw = {}
            tpb = [4, 5] if ut_dst == "sbuf" else [2, 3, 4, 5]
            ntpb = len(tpb)

            for c in range(NCH):
                for nt in range(NT_CH):
                    g = c * NT_CH + nt
                    sl = (g % 4) * 48
                    w = []
                    if g >= 4:
                        w.append(("ln_dve", norm_done[g - 4]))
                    if xwait_fn is not None:
                        w += xwait_fn(c, nt)
                    src = src_hbm[g * 128:(g + 1) * 128, :]
                    xsem = f"d_xst{g % 4}"
                    p.add("sync",
                          lambda e, d=xst[g % 4], s=src: e.dma_start(out=d, in_=s),
                          waits=w, inc=(xsem, 16), tag=f"{phase}xld{g}")
                    xdma[g] = (xsem, p.cnt[xsem])

                    ws = [xdma[g]]
                    if g >= 4:
                        ws.append(("ln_dve", pack_done[g - 4]))
                    p.add("vector",
                          lambda e, o=stat[:, sl:sl + 6],
                          i=xst[g % 4][:, 0:512]:
                          e.bn_stats(o, i),
                          waits=ws, inc=("ln_dve", 1), tag=f"{phase}bna{g}")
                    p.add("vector",
                          lambda e, o=stat[:, sl + 8:sl + 14],
                          i=xst[g % 4][:, 512:1024]:
                          e.bn_stats(o, i),
                          waits=[xdma[g]], inc=("ln_dve", 1),
                          tag=f"{phase}bnb{g}")
                    bn_done[g] = p.cnt["ln_dve"]

                    wp = [("ln_dve", bn_done[g])]
                    if g >= 4:
                        wp.append(("ln_dve", aggr_done[g - 4]))
                    p.add("vector",
                          lambda e,
                          o=stat[:, sl + 16:sl + 28].rearrange(
                              "p (r k) -> p r k", k=6),
                          i=stat[:, sl:sl + 16].rearrange(
                              "p (r k) -> p r k", k=8)[:, :, 0:6]:
                          e.tensor_copy(o, i),
                          waits=wp, inc=("ln_dve", 1), tag=f"{phase}pk{g}")
                    pack_done[g] = p.cnt["ln_dve"]

                    wa = [("ln_dve", pack_done[g])]
                    if g >= 4:
                        wa.append(("ln_act", sqrt_done[g - 4]))
                        wa.append(("ln_dve", norm_done[g - 4]))
                    p.add("vector",
                          lambda e, o=stat[:, sl + 32:sl + 34],
                          i=stat[:, sl + 16:sl + 28]:
                          e.bn_aggr(o, i),
                          waits=wa, inc=("ln_dve", 1), tag=f"{phase}agg{g}")
                    aggr_done[g] = p.cnt["ln_dve"]

                    wsq = [("ln_dve", aggr_done[g])]
                    if g >= 4:
                        wsq.append(("ln_dve", recip_done[g - 4]))
                    p.add("scalar",
                          lambda e, o=stat[:, sl + 36:sl + 37],
                          i=stat[:, sl + 33:sl + 34]:
                          e.activation(o, i, AF.Sqrt, bias=np.float32(EPS)),
                          waits=wsq, inc=("ln_act", 1), tag=f"{phase}sqrt{g}")
                    sqrt_done[g] = p.cnt["ln_act"]

                    wr = [("ln_act", sqrt_done[g])]
                    if g >= 4:
                        wr.append(("ln_dve", norm_done[g - 4]))
                    p.add("vector",
                          lambda e, o=stat[:, sl + 40:sl + 41],
                          i=stat[:, sl + 36:sl + 37]:
                          e.reciprocal(o, i),
                          waits=wr, inc=("ln_dve", 1), tag=f"{phase}rcp{g}")
                    recip_done[g] = p.cnt["ln_dve"]

                    w = [("ln_dve", recip_done[g])]
                    if g >= 4:
                        w.append(("ln_mm", tp_done[g - 4]))
                    p.add("vector",
                          lambda e, o=ubf[g % 4], i=xst[g % 4],
                          mn=stat[:, sl + 32:sl + 33],
                          rs=stat[:, sl + 40:sl + 41]:
                          e.tensor_scalar(o, i, mn, rs, ALU.subtract, ALU.mult),
                          waits=w, inc=("ln_dve", 1), tag=f"{phase}nrm{g}")
                    norm_done[g] = p.cnt["ln_dve"]

                    bank = pt[tpb[g % ntpb]][:].bitcast(BF16)
                    for dt in range(NDT):
                        w = [("ln_dve", norm_done[g])] if dt == 0 else []
                        if dt == 0:
                            w.append(K["ident"])
                            if g >= ntpb:
                                w.append(("ln_act", evac_done[g - ntpb]))
                        p.add("tensor",
                              lambda e, o=bank[:, dt * 128:(dt + 1) * 128],
                              i=ubf[g % 4][:, dt * 128:(dt + 1) * 128]:
                              e.transpose(o, i, ident),
                              waits=w,
                              inc=("ln_mm", 1) if dt == NDT - 1 else None,
                              tag=f"{phase}tp{g}.{dt}")
                    tp_done[g] = p.cnt["ln_mm"]

                    def emit_tileB(gB):
                        cB, ntB = divmod(gB, NT_CH)
                        bankB = pt[tpb[gB % ntpb]][:].bitcast(BF16)
                        if ut_dst == "hbm":
                            dsts = uts.rearrange(
                                "p (dt t) -> p dt t", dt=NDT)[
                                :, :, ntB * 128:(ntB + 1) * 128]
                        else:
                            dsts = at_uT.rearrange(
                                "p (dt n) -> p dt n", dt=NDT)[
                                :, :, gB * 128:(gB + 1) * 128]
                        w = [("ln_mm", tp_done[gB])]
                        if ut_dst == "hbm" and cB >= 1:
                            w.append(utw[cB - 1])
                            if gate:
                                w.append(("gate_mm",
                                          gmm_done[(cB - 1) * NT_CH + ntB]))
                        p.add("scalar",
                              lambda e, o=dsts,
                              i=bankB.rearrange("p (dt k) -> p dt k", dt=NDT):
                              e.activation(o, i, AF.Identity),
                              waits=w, inc=("ln_act", 1), tag=f"{phase}ev{gB}")
                        evac_done[gB] = p.cnt["ln_act"]

                        if ut_dst == "hbm" and ntB == NT_CH - 1:
                            usem = f"d_utw{cB % 4}"
                            p.add("scalar",
                                  lambda e, s=uts,
                                  d=uT_hbm[:, cB * 4096:(cB + 1) * 4096]:
                                  e.dma_start(out=d, in_=s),
                                  waits=[("ln_act", evac_done[gB])],
                                  inc=(usem, 16), tag=f"{phase}ud{cB}")
                            utw[cB] = (usem, p.cnt[usem])
                            K[("ut", ph, cB)] = utw[cB]

                        if gate:
                            for dt in range(NDT):
                                w = []
                                if dt == 0:
                                    w = [("ln_act", evac_done[gB]),
                                         K["wg"], K["bg"],
                                         ("gp_c", K["ones"])]
                                p.add("tensor",
                                      lambda e,
                                      o=pt[6][:, gB * 8:(gB + 1) * 8],
                                      l=uts[:, dt * 512 + ntB * 128:
                                            dt * 512 + (ntB + 1) * 128],
                                      r=wg_sb[:, dt * 8:(dt + 1) * 8],
                                      st=(dt == 0):
                                      e.matmul(o, l, r, start=st, stop=False),
                                      waits=w, tag=f"{phase}g{gB}.{dt}")
                            p.add("tensor",
                                  lambda e, o=pt[6][:, gB * 8:(gB + 1) * 8]:
                                  e.matmul(o, ones_row, bg_row, start=False,
                                           stop=True),
                                  inc=("gate_mm", 1), tag=f"{phase}gb{gB}")
                            gmm_done[gB] = p.cnt["gate_mm"]

                    if ut_dst == "hbm":
                        if g >= 1:
                            emit_tileB(g - 1)
                        if g == NNT - 1:
                            emit_tileB(g)
                    else:
                        emit_tileB(g)

                if chunk_cb is not None:
                    chunk_cb(c, dict(evac_done=evac_done, norm_done=norm_done,
                                     tp_done=tp_done))

            if gate:
                emit_gate_softmax(phase, gmm_done[NNT - 1])
            if ut_dst == "hbm":
                K[("utw_final", ph)] = [(f"d_utw{i}", p.cnt[f"d_utw{i}"])
                                        for i in range(4)]
            return dict(ph=ph, norm_done=norm_done, evac_done=evac_done,
                        xdma=xdma)

        def emit_gate_softmax(phase, mm_cnt):
            v3 = lambda ap: ap.rearrange("p (g k) -> p g k", k=E)
            S0 = lambda: gs[:, 0:32]
            S1 = lambda: gs[:, 32:64]

            def dve(fn, waits=(), tag=""):
                p.add("vector", fn, waits=waits, inc=("gate_dve", 1),
                      tag=phase + tag)
                return p.cnt["gate_dve"]

            g3 = lambda col: col.rearrange("p (g o) -> p g o", o=1)
            bc = lambda col: g3(col).broadcast_to([128, 32, E])

            mxc = gs[:, 64:96]
            c1 = dve(lambda e: e.reduce_max(g3(mxc), v3(pt[6][:, 0:256]),
                                            axis=AX.X),
                     waits=[("gate_mm", mm_cnt)], tag="gmx")
            c2 = dve(lambda e: e.tensor_tensor(v3(gt[0]), v3(pt[6][:, 0:256]),
                                               bc(mxc), ALU.subtract),
                     waits=[("gate_dve", c1)], tag="gxm")
            p.add("scalar", lambda e: e.activation(gt[1], gt[0], AF.Exp),
                  waits=[("gate_dve", c2)], inc=("ln_act", 1),
                  tag=phase + "gexp")
            ce = p.cnt["ln_act"]
            c4 = dve(lambda e: e.reduce_sum(g3(S0()), v3(gt[1]), axis=AX.X),
                     waits=[("ln_act", ce)], tag="gss")
            c5 = dve(lambda e: e.reciprocal(S1(), S0()),
                     waits=[("gate_dve", c4)], tag="grs")
            c6 = dve(lambda e: e.tensor_tensor(v3(gt[2]), v3(gt[1]), bc(S1()),
                                               ALU.mult),
                     waits=[("gate_dve", c5)], tag="gpr")
            c7 = dve(lambda e: e.reduce_max(g3(S0()), v3(gt[2]), axis=AX.X),
                     waits=[("gate_dve", c6)], tag="gm1")
            c8 = dve(lambda e: e.tensor_tensor(v3(gt[0]), v3(gt[2]), bc(S0()),
                                               ALU.is_ge),
                     waits=[("gate_dve", c7)], tag="gmk1")
            c9 = dve(lambda e: e.reduce_sum(g3(S1()), v3(gt[0]), axis=AX.X),
                     waits=[("gate_dve", c8)], tag="gc1")
            c10 = dve(lambda e: e.scalar_tensor_tensor(
                gt[1], gt[0], -1.0e9, gt[2], ALU.mult, ALU.add),
                waits=[("gate_dve", c9)], tag="gpm")
            c11 = dve(lambda e: e.reduce_max(g3(mxc), v3(gt[1]), axis=AX.X),
                      waits=[("gate_dve", c10)], tag="gm2")
            c12 = dve(lambda e: e.tensor_tensor(v3(gt[3]), v3(gt[1]), bc(mxc),
                                                ALU.is_ge),
                      waits=[("gate_dve", c11)], tag="gmk2")
            c13 = dve(lambda e: e.tensor_scalar(mxc, S1(), 1.5, None,
                                                ALU.is_le),
                      waits=[("gate_dve", c12)], tag="gn2")
            c14a = dve(lambda e: e.tensor_tensor(v3(gt[3]), v3(gt[3]), bc(mxc),
                                                 ALU.mult),
                       waits=[("gate_dve", c13)], tag="gmka")
            c14 = dve(lambda e: e.tensor_tensor(gt[1], gt[3], gt[0], ALU.add),
                      waits=[("gate_dve", c14a)], tag="gmkb")
            c15 = dve(lambda e: e.tensor_tensor(gt[0], gt[2], gt[1], ALU.mult),
                      waits=[("gate_dve", c14)], tag="gsr")
            c16 = dve(lambda e: e.reduce_sum(g3(S0()), v3(gt[0]), axis=AX.X),
                      waits=[("gate_dve", c15)], tag="gdn")
            c17 = dve(lambda e: e.tensor_scalar(S1(), S0(), 1e-8, None,
                                                ALU.add),
                      waits=[("gate_dve", c16)], tag="gdp")
            c18 = dve(lambda e: e.reciprocal(S0(), S1()),
                      waits=[("gate_dve", c17)], tag="grd")
            c19 = dve(lambda e: e.tensor_scalar(S1(), S0(), 0.5, None,
                                                ALU.mult),
                      waits=[("gate_dve", c18)], tag="grd5")
            c20 = dve(lambda e: e.tensor_tensor(v3(sel_sb), v3(gt[0]), bc(S1()),
                                                ALU.mult),
                      waits=[("gate_dve", c19)], tag="gsel")
            K[("sel", phase)] = c20
            c21 = dve(lambda e: e.tensor_copy(selbf, sel_sb),
                      waits=[("gate_dve", c20)], tag="gselbf")
            K[("selbf", phase)] = c21

        def emit_ffn(call, xsrc_hbm, xdst_hbm, xfin_hbm, selphase, ln_info,
                     wregion_waits=(), xtra_first_waits=(), xsrc_wait=None,
                     phase=""):
            s1_done, gelu_done, s2_done, evac_done = {}, {}, {}, {}
            xout_dma, xin_dma, ut_dma = {}, {}, {}
            w2f_base = p.cnt["w2f"]
            lnph = ln_info["ph"]

            for q in range(8):
                load_w1q(call, 0, q, list(wregion_waits) if q == 0 else [],
                         f"{phase}w1.0.{q}")
            for q in range(8):
                load_w2q(call, 0, q, [], f"{phase}w2.0.{q}")

            def emit_utload(cj):
                cc = cj % NCH
                w = [K[("ut", lnph, cc)]]
                if cj >= 2:
                    w.append(("s1", s1_done[(cj - 2) * NHB + NHB - 1]))
                if cj <= 1:
                    w += list(xtra_first_waits)
                    w.append(("gate_dve", K[("sel", selphase)]))
                    w += K[("utw_final", lnph)]
                usem = f"d_utb{cc % 2}"
                p.add("sync",
                      lambda e, d=uTb[cc % 2],
                      s=uT_hbm[:, cc * 4096:(cc + 1) * 4096]:
                      e.dma_start(out=d, in_=s),
                      waits=w, inc=(usem, 16), tag=f"{phase}utd{cj}")
                ut_dma[cj] = (usem, p.cnt[usem])

            emit_utload(0)
            emit_utload(1)
            for e_ in range(E):
                for c in range(NCH):
                    ci = e_ * NCH + c

                    def emit_xin(nt):
                        g = c * NT_CH + nt
                        w = []
                        if e_ > 0:
                            w.append(xout_dma[(e_ - 1, c, nt)])
                        elif xsrc_wait is not None:
                            w += xsrc_wait(c, nt)
                        prev = prev_nt(e_, c, nt)
                        if prev is not None:
                            w.append(("evac", evac_done[(prev[0], prev[1],
                                                         prev[2], 1)]))
                        if (e_, c, nt) == (0, 0, 0):
                            w += list(xtra_first_waits)
                        src = (xsrc_hbm if e_ == 0 else xdst_hbm)[
                            g * 128:(g + 1) * 128, :]
                        xs = f"d_xin{nt % 2}"
                        p.add("sync",
                              lambda e, d=xinb[nt % 2], s=src:
                              e.dma_start(out=d, in_=s),
                              waits=w, inc=(xs, 16),
                              tag=f"{phase}xin{e_}.{c}.{nt}")
                        xin_dma[(e_, c, nt)] = (xs, p.cnt[xs])

                    for nt in range(2):
                        emit_xin(nt)

                    for hb in range(NHB):
                        gi = ci * NHB + hb
                        w = []
                        if hb == 0:
                            w.append(ut_dma[ci])
                        if hb % 4 == 0:
                            w.append(K[("w1", call, e_, hb // 4)])
                        if gi >= 2:
                            w.append(("gelu", gelu_done[gi - 2]))
                        w1base = (hb // 4) * 4096 + (hb % 4) * 128
                        for dt in range(NDT):
                            p.add("tensor",
                                  lambda e, o=pt[hb % 2][:],
                                  l=w1sb[:, w1base + dt * 512:
                                         w1base + dt * 512 + 128],
                                  r=uTb[c % 2][:, dt * CH:(dt + 1) * CH],
                                  st=(dt == 0), sp=(dt == NDT - 1):
                                  e.matmul(o, l, r, start=st, stop=sp),
                                  waits=w if dt == 0 else [],
                                  inc=("s1", 1) if dt == NDT - 1 else None,
                                  tag=f"{phase}s1.{e_}.{c}.{hb}.{dt}")
                        s1_done[gi] = p.cnt["s1"]

                        p.add("scalar",
                              lambda e, o=hT[:, hb * CH:(hb + 1) * CH],
                              i=pt[hb % 2][:],
                              b=b1col[:, e_ * NHB + hb:e_ * NHB + hb + 1]:
                              e.activation(o, i, AF.Gelu_apprx_tanh, bias=b),
                              waits=[("s1", s1_done[gi]), K["b1col"]],
                              inc=("gelu", 1), tag=f"{phase}gl.{e_}.{c}.{hb}")
                        gelu_done[gi] = p.cnt["gelu"]

                    if ci + 2 < E * NCH:
                        emit_utload(ci + 2)

                    lastchunk = (c == NCH - 1)
                    pass_idx = ci * 4
                    if e_ == E - 1:
                        ps0bf = pt[0][:].bitcast(BF16)
                        for nt in range(NT_CH):
                            w = [("gelu", gelu_done[ci * NHB + 30])]
                            if nt == 0:
                                w.append(("gate_dve",
                                          K[("selbf", selphase)]))
                                if ("corr_done", c - 1) in K:
                                    w.append(K[("corr_done", c - 1)])
                            p.add("tensor",
                                  lambda e,
                                  o=ps0bf[0:8, nt * 128:(nt + 1) * 128],
                                  i=selbf[:, (c * NT_CH + nt) * 8:
                                          (c * NT_CH + nt + 1) * 8]:
                                  e.transpose(o, i, ident),
                                  waits=w,
                                  inc=("ln_mm", 1) if nt == NT_CH - 1
                                  else None,
                                  tag=f"{phase}selt.{c}.{nt}")
                        seltp = p.cnt["ln_mm"]
                        p.add("scalar",
                              lambda e, o=selT, i=ps0bf[0:8, 0:512]:
                              e.activation(o, i, AF.Identity),
                              waits=[("ln_mm", seltp)],
                              inc=("ln_act", 1), tag=f"{phase}selte.{c}")
                        K[("selte", c)] = ("ln_act", p.cnt["ln_act"])
                    for half in range(2):
                        for fh in range(2):
                            pi = pass_idx + half * 2 + fh
                            b0 = 2 if pi % 2 == 0 else 4
                            nts = (0, 1) if half == 0 else (2, 3)
                            w = []
                            if half == 0 and fh == 0:
                                w.append(("gelu",
                                          gelu_done[ci * NHB + NHB - 1]))
                                if ci == 0:
                                    w.append(("ln_act",
                                              ln_info["evac_done"][NNT - 1]))
                            if pi >= 2:
                                pp = pi - 2
                                ph_, pf_ = (pp % 4) // 2, pp % 2
                                pci = pp // 4
                                pnts = (0, 1) if ph_ == 0 else (2, 3)
                                w.append(("evac",
                                          evac_done[(pci // NCH, pci % NCH,
                                                     pnts[1], pf_)]))
                            lastpass = lastchunk and half == 1 and fh == 1
                            for hb in range(NHB):
                                ww = list(w) if hb == 0 else []
                                if hb % 4 == 0:
                                    ww.append(K[("w2", call, e_, hb // 4)])
                                for j, nt in enumerate(nts):
                                    p.add("tensor",
                                          lambda e, o=pt[b0 + j][:],
                                          l=hT[:, hb * CH + nt * 128:
                                               hb * CH + (nt + 1) * 128],
                                          r=w2sb[:, hb * 1024 + fh * 512:
                                                 hb * 1024 + fh * 512 + 512],
                                          st=(hb == 0),
                                          sp=(hb == NHB - 1):
                                          e.matmul(o, l, r, start=st,
                                                   stop=sp),
                                          waits=ww if j == 0 else [],
                                          inc=("s2", 1)
                                          if hb == NHB - 1 else
                                          (("w2f", 1)
                                           if (lastpass and j == 1
                                               and hb < NHB - 1) else None),
                                          tag=f"{phase}s2.{e_}.{c}.{pi}.{hb}.{nt}")
                                    if hb == NHB - 1:
                                        s2_done[(e_, c, nt, fh)] = \
                                            p.cnt["s2"]
                            if e_ == E - 1:
                                for j, nt in enumerate(nts):
                                    w = [K[("selte", c)],
                                         ("gelu",
                                          gelu_done[ci * NHB + NHB - 1]),
                                         K["b2r"]]
                                    if ("correv", c, nts[0], fh) in K or pi >= 2:
                                        pass
                                    prevc = K.get(("correv_prev", j))
                                    if prevc is not None:
                                        w.append(prevc)
                                    p.add("tensor",
                                          lambda e, o=pt[6 + j][:],
                                          l=selT[0:8, nt * 128:(nt + 1) * 128],
                                          r=b2r[0:8, fh * 512:fh * 512 + 512]:
                                          e.matmul(o, l, r, start=True,
                                                   stop=True),
                                          waits=w, inc=("corr", 1),
                                          tag=f"{phase}corr.{e_}.{c}.{nt}.{fh}")
                                    K[("corrmm", c, nt, fh)] = \
                                        ("corr", p.cnt["corr"])
                            for j, nt in enumerate(nts):
                                selcol = sel_sb[:, (c * NT_CH + nt) * E + e_:
                                                (c * NT_CH + nt) * E + e_ + 1]
                                wv = [("s2", s2_done[(e_, c, nt, fh)]),
                                      xin_dma[(e_, c, nt)]]
                                prev = prev_nt(e_, c, nt)
                                if prev is not None and (prev[0], prev[1],
                                                         prev[2]) in xout_dma:
                                    wv.append(xout_dma[(prev[0], prev[1],
                                                        prev[2])])
                                p.add("vector",
                                      lambda e,
                                      o=xoutb[nt % 2][:, fh * 512:
                                                      fh * 512 + 512],
                                      i0=pt[b0 + j][:], sc=selcol,
                                      i1=xinb[nt % 2][:, fh * 512:
                                                      fh * 512 + 512]:
                                      e.scalar_tensor_tensor(o, i0, sc, i1,
                                                             ALU.mult,
                                                             ALU.add),
                                      waits=wv,
                                      inc=("ev1", 1) if e_ == E - 1 else
                                      ("evac", 1),
                                      tag=f"{phase}ev.{e_}.{c}.{nt}.{fh}")
                                if e_ == E - 1:
                                    ev1c = p.cnt["ev1"]
                                    p.add("vector",
                                          lambda e,
                                          o=xoutb[nt % 2][:, fh * 512:
                                                          fh * 512 + 512],
                                          b=pt[6 + j][:]:
                                          e.tensor_tensor(o, o, b, ALU.add),
                                          waits=[K[("corrmm", c, nt, fh)],
                                                 ("ev1", ev1c)],
                                          inc=("evac", 1),
                                          tag=f"{phase}ev2.{e_}.{c}.{nt}.{fh}")
                                    K[("correv_prev", j)] = \
                                        ("evac", p.cnt["evac"])
                                evac_done[(e_, c, nt, fh)] = p.cnt["evac"]
                        if half == 0:
                            for nt in nts:
                                g = c * NT_CH + nt
                                dsth = (xfin_hbm if e_ == E - 1 else
                                        xdst_hbm)[g * 128:(g + 1) * 128, :]
                                xo = f"d_xout{nt % 2}"
                                p.add("sync",
                                      lambda e, d=dsth, s=xoutb[nt % 2]:
                                      e.dma_start(out=d, in_=s),
                                      waits=[("evac",
                                              evac_done[(e_, c, nt, 1)])],
                                      inc=(xo, 16),
                                      tag=f"{phase}xo{e_}.{c}.{nt}")
                                xout_dma[(e_, c, nt)] = (xo, p.cnt[xo])
                                emit_xin(nt + 2)
                    if e_ == E - 1:
                        K[("corr_done", c)] = ("evac",
                                               evac_done[(e_, c, 3, 1)])

                    for nt in (2, 3):
                        g = c * NT_CH + nt
                        dst = (xfin_hbm if e_ == E - 1 else xdst_hbm)[
                            g * 128:(g + 1) * 128, :]
                        xo = f"d_xout{nt % 2}"
                        p.add("sync",
                              lambda e, d=dst, s=xoutb[nt % 2]:
                              e.dma_start(out=d, in_=s),
                              waits=[("evac", evac_done[(e_, c, nt, 1)])],
                              inc=(xo, 16),
                              tag=f"{phase}xo{e_}.{c}.{nt}")
                        xout_dma[(e_, c, nt)] = (xo, p.cnt[xo])

                    if lastchunk and e_ < E - 1:
                        for q in range(8):
                            load_w1q(call, e_ + 1, q,
                                     [("s1", s1_done[ci * NHB + 4 * q + 3])],
                                     f"{phase}w1.{e_ + 1}.{q}")
                        for q in range(8):
                            if q < 7:
                                ww2 = [("w2f",
                                        w2f_base + e_ * 31 + 4 * q + 4)]
                            else:
                                ww2 = [("s2", s2_done[(e_, c, NT_CH - 1, 1)])]
                            load_w2q(call, e_ + 1, q, ww2,
                                     f"{phase}w2.{e_ + 1}.{q}")


            return dict(s1_done=s1_done, gelu_done=gelu_done, s2_done=s2_done,
                        evac_done=evac_done, xout_dma=xout_dma)

        def prev_nt(e_, c, nt):
            if nt >= 2:
                return (e_, c, nt - 2)
            if c > 0:
                return (e_, c - 1, nt + NT_CH - 2)
            if e_ > 0:
                return (e_ - 1, NCH - 1, nt + NT_CH - 2)
            return None

        def emit_attn(ffn1, xwait_fn, phase="A."):
            mm, act, dv = {}, {}, {}
            s2f = ffn1["s2_done"][(E - 1, NCH - 1, NT_CH - 1, 1)]
            evf = ffn1["evac_done"][(E - 1, NCH - 1, NT_CH - 1, 1)]
            glf = ffn1["gelu_done"][E * NCH * NHB - 1]
            dma_finals = [(s, p.cnt[s]) for s in
                          ("d_xout0", "d_xout1", "d_xin0", "d_xin1",
                           "d_utb0", "d_utb1")]

            p.add("gpsimd", lambda e: e.dma_start(
                out=at_wqkv.rearrange("p (dt j) -> p dt j", dt=NDT),
                in_=wqkv_in.rearrange("(dt p) j -> p dt j", p=128)),
                waits=[("s2", s2f)], inc=("d_wqkv", 16), tag=phase + "wqkv")
            K["wqkv"] = ("d_wqkv", p.cnt["d_wqkv"])
            p.add("gpsimd", lambda e: e.dma_start(
                out=at_wout.rearrange("p (jt f) -> p jt f", jt=NDT),
                in_=wout_in.rearrange("(jt p) f -> p jt f", p=128)),
                waits=[("s2", s2f)], inc=("d_wout", 16), tag=phase + "wout")
            K["wout"] = ("d_wout", p.cnt["d_wout"])

            def pass1_chunk(c, ln):
                lnrdy = ln["evac_done"][c * NT_CH + NT_CH - 1]
                for jt in range(NDT):
                    w = []
                    if jt == 0:
                        w += [("ln_act", lnrdy), K["wqkv"]]
                        if c == 0:
                            w += [("gelu", glf), ("evac", evf)] + dma_finals
                    kk = ("kt", c, jt)
                    prev = ("kt", c, jt - 2) if jt >= 2 else (
                        ("v", c - 1, 6 + jt) if c > 0 else None)
                    if prev is not None and prev in act:
                        w.append(("at_act", act[prev]))
                    bank = pt[jt % 2]
                    for dt in range(NDT):
                        p.add("tensor",
                              lambda e, o=bank[:],
                              l=at_wqkv[:, dt * 3072 + 1024 + jt * 128:
                                        dt * 3072 + 1024 + (jt + 1) * 128],
                              r=at_uT[:, dt * 4096 + c * CH:
                                      dt * 4096 + (c + 1) * CH],
                              st=(dt == 0), sp=(dt == NDT - 1):
                              e.matmul(o, l, r, start=st, stop=sp),
                              waits=w if dt == 0 else [],
                              inc=("at_mm", 1) if dt == NDT - 1 else None,
                              tag=f"{phase}kt.{c}.{jt}.{dt}")
                    mm[kk] = p.cnt["at_mm"]
                    w2 = [("at_mm", mm[kk])]
                    if c > 0:
                        w2.append(("at_mm", mm[("phik", c - 1, 7)]))
                    p.add("scalar",
                          lambda e, o=at_kT[:, jt * 512:(jt + 1) * 512],
                          i=bank[:]: e.activation(o, i, AF.Identity),
                          waits=w2, inc=("at_act", 1),
                          tag=f"{phase}kte.{c}.{jt}")
                    act[kk] = p.cnt["at_act"]

                for nt in range(NT_CH):
                    for hf in range(2):
                        vi = nt * 2 + hf
                        w = []
                        prev = ("v", c, vi - 2) if vi >= 2 else \
                            ("kt", c, NDT - 2 + vi)
                        if prev in act:
                            w.append(("at_act", act[prev]))
                        bank = pt[vi % 2]
                        for dt in range(NDT):
                            p.add("tensor",
                                  lambda e, o=bank[:],
                                  l=at_uT[:, dt * 4096 + (c * NT_CH + nt) * 128:
                                          dt * 4096 + (c * NT_CH + nt + 1) * 128],
                                  r=at_wqkv[:, dt * 3072 + 2048 + hf * 512:
                                            dt * 3072 + 2048 + hf * 512 + 512],
                                  st=(dt == 0), sp=(dt == NDT - 1):
                                  e.matmul(o, l, r, start=st, stop=sp),
                                  waits=w if dt == 0 else [],
                                  inc=("at_mm", 1) if dt == NDT - 1 else None,
                                  tag=f"{phase}v.{c}.{vi}.{dt}")
                        mm[("v", c, vi)] = p.cnt["at_mm"]
                        w2 = [("at_mm", mm[("v", c, vi)])]
                        if c > 0:
                            w2.append(("at_mm", mm[("kv", c - 1, PAIRS - 1)]))
                        p.add("scalar",
                              lambda e,
                              o=at_v[:, nt * 1024 + hf * 512:
                                     nt * 1024 + hf * 512 + 512],
                              i=bank[:]: e.activation(o, i, AF.Identity),
                              waits=w2, inc=("at_act", 1),
                              tag=f"{phase}ve.{c}.{vi}")
                        act[("v", c, vi)] = p.cnt["at_act"]

                for nt in range(NT_CH):
                    for half in range(2):
                        bank = pt[2 + half]
                        w = []
                        prev = ("phik", c, (nt - 1) * 2 + half) if nt >= 1 \
                            else (("phik", c - 1, 6 + half) if c > 0 else None)
                        if prev is not None and prev in act:
                            w.append(("at_act", act[prev]))
                        if c == 0 and nt == 0:
                            w.append(K["rf"])
                        for pr4 in range(4):
                            pr = half * 4 + pr4
                            p.add("tensor",
                                  lambda e, o=bank[:, pr4 * 128:(pr4 + 1) * 128],
                                  l=at_kT[:, pr * 512 + nt * 128:
                                          pr * 512 + (nt + 1) * 128],
                                  r=rf_sb[:, pr * 128:(pr + 1) * 128]:
                                  e.matmul(o, l, r, start=True, stop=True),
                                  waits=(w + [("at_act", act[("kt", c, pr)])])
                                  if pr4 == 0 else
                                  [("at_act", act[("kt", c, pr)])],
                                  inc=("at_mm", 1) if pr4 == 3 else None,
                                  tag=f"{phase}pk.{c}.{nt}.{pr}")
                        mm[("phik", c, nt * 2 + half)] = p.cnt["at_mm"]
                        wpe = [("at_mm", mm[("phik", c, nt * 2 + half)])]
                        if c > 0 and nt == 0:
                            wpe.append(("at_mm", mm[("kv", c - 1, PAIRS - 1)]))
                        p.add("scalar",
                              lambda e,
                              o=at_phik[:, nt * 1024 + half * 512:
                                        nt * 1024 + half * 512 + 512],
                              i=bank[:]: e.activation(o, i, AF.Relu),
                              waits=wpe,
                              inc=("at_act", 1),
                              tag=f"{phase}pke.{c}.{nt}.{half}")
                        act[("phik", c, nt * 2 + half)] = p.cnt["at_act"]

                for pr in range(PAIRS):
                    bank = pt[6 + pr % 2]
                    for nt in range(NT_CH):
                        w = [("at_act", act[("phik", c, nt * 2 + pr // 4)]),
                             ("at_act", act[("v", c, nt * 2 + 0)]),
                             ("at_act", act[("v", c, nt * 2 + 1)])]
                        if nt == 0:
                            bprev = (c, pr - 2) if pr >= 2 else (c - 1, 6 + pr)
                            if ("kvadd", bprev[0], bprev[1]) in dv:
                                w.append(("at_dve",
                                          dv[("kvadd", bprev[0], bprev[1])]))
                            if c == 0 and pr < 2:
                                w.append(("evac", evf))
                                w.append(("gate_dve", K[("sel", "L1.")]))
                        p.add("tensor",
                              lambda e, o=bank[:, 0:128],
                              l=at_phik[:, nt * 1024 + pr * 128:
                                        nt * 1024 + (pr + 1) * 128],
                              r=at_v[:, nt * 1024 + pr * 128:
                                     nt * 1024 + (pr + 1) * 128],
                              st=(nt == 0):
                              e.matmul(o, l, r, start=st, stop=False),
                              waits=w, tag=f"{phase}kv.{c}.{pr}.{nt}")
                        wks = []
                        if c == 0 and pr == 0 and nt == 0:
                            wks = [("gp_c", K["ones"])]
                        p.add("tensor",
                              lambda e, o=bank[:, 128:129],
                              l=at_phik[:, nt * 1024 + pr * 128:
                                        nt * 1024 + (pr + 1) * 128],
                              sp=(nt == NT_CH - 1):
                              e.matmul(o, l, ones_col, start=False, stop=sp),
                              waits=wks,
                              inc=("at_mm", 1) if nt == NT_CH - 1 else None,
                              tag=f"{phase}ks.{c}.{pr}.{nt}")
                    mm[("kv", c, pr)] = p.cnt["at_mm"]
                    dst = kv_acc[:, pr * 132:pr * 132 + 129]
                    if c == 0:
                        p.add("vector",
                              lambda e, o=dst, i=bank[:, 0:129]:
                              e.tensor_copy(o, i),
                              waits=[("at_mm", mm[("kv", c, pr)])],
                              inc=("at_dve", 1), tag=f"{phase}kva.{c}.{pr}")
                    else:
                        p.add("vector",
                              lambda e, o=dst, a=dst, b=bank[:, 0:129]:
                              e.tensor_tensor(o, a, b, ALU.add),
                              waits=[("at_mm", mm[("kv", c, pr)])],
                              inc=("at_dve", 1), tag=f"{phase}kva.{c}.{pr}")
                    dv[("kvadd", c, pr)] = p.cnt["at_dve"]

            ln2 = emit_ln(xbuf, gate=False, ut_dst="sbuf", xwait_fn=xwait_fn,
                          phase="L2.", chunk_cb=pass1_chunk)
            ln2f = ln2["evac_done"][NNT - 1]

            kvf = p.cnt["at_dve"]
            kvw = [("at_dve", dv[("kvadd", NCH - 1, pq)]) for pq in range(PAIRS)]
            kvmmf = mm[("kv", NCH - 1, PAIRS - 1)]
            for pr in range(PAIRS):
                for q in range(2):
                    p.add("scalar",
                          lambda e,
                          o=kv_bd[64 * q:64 * (q + 1),
                                  pr * 128 + 64 * q:pr * 128 + 64 * (q + 1)],
                          i=kv_acc[64 * q:64 * (q + 1),
                                   pr * 132 + 64 * q:pr * 132 + 64 * (q + 1)]:
                          e.activation(o, i, AF.Identity),
                          waits=(kvw + [("gp_c", K["ones"])])
                          if (pr, q) == (0, 0) else [],
                          inc=("at_act", 1), tag=f"{phase}kvbd.{pr}.{q}")
            act["kvbd"] = p.cnt["at_act"]
            p.add("vector",
                  lambda e: e.tensor_tensor(
                      ksum_bd.rearrange("p (a q) -> p a q", q=2),
                      kv_acc.rearrange("p (a k) -> p a k", k=132)[:, :, 128:129]
                      .broadcast_to([128, 8, 2]),
                      ksum_qm.rearrange("p (a q) -> p a q", q=2),
                      ALU.mult),
                  waits=[K["qmask"]] + kvw,
                  inc=("at_dve", 1), tag=phase + "ksbd")
            dv["ksbd"] = p.cnt["at_dve"]

            for c in range(NCH):
                for jt in range(NDT):
                    w = []
                    if c == 0 and jt < 2:
                        w.append(("at_act", act[("v", NCH - 1, 6 + jt)]))
                    if c > 0 and jt < 2:
                        if jt == 0:
                            w.append(("at_dve", dv[("attnT", c - 1, PAIRS - 1)]))
                        else:
                            w.append(("at_act", act[("nes", c - 1, PAIRS - 1)]))
                    prev = ("qt", c, jt - 2) if jt >= 2 else None
                    if prev is not None and prev in act:
                        w.append(("at_act", act[prev]))
                    bank = pt[jt % 2]
                    for dt in range(NDT):
                        p.add("tensor",
                              lambda e, o=bank[:],
                              l=at_wqkv[:, dt * 3072 + jt * 128:
                                        dt * 3072 + (jt + 1) * 128],
                              r=at_uT[:, dt * 4096 + c * CH:
                                      dt * 4096 + (c + 1) * CH],
                              st=(dt == 0), sp=(dt == NDT - 1):
                              e.matmul(o, l, r, start=st, stop=sp),
                              waits=w if dt == 0 else [],
                              inc=("at_mm", 1) if dt == NDT - 1 else None,
                              tag=f"{phase}qt.{c}.{jt}.{dt}")
                    mm[("qt", c, jt)] = p.cnt["at_mm"]
                    wq = [("at_mm", mm[("qt", c, jt)])]
                    if c == 0:
                        wq.append(("at_mm", kvmmf))
                    p.add("scalar",
                          lambda e, o=at_qT[:, jt * 512:(jt + 1) * 512],
                          i=bank[:]: e.activation(o, i, AF.Identity),
                          waits=wq, inc=("at_act", 1),
                          tag=f"{phase}qte.{c}.{jt}")
                    act[("qt", c, jt)] = p.cnt["at_act"]

                for pr in range(PAIRS):
                    bank = pt[2 + pr % 2]
                    w = [("at_act", act[("qt", c, pr)])]
                    prev = ("phiq", c, pr - 2) if pr >= 2 else (
                        ("phiq", c - 1, PAIRS - 2 + pr) if c > 0 else None)
                    if prev is not None and prev in act:
                        w.append(("at_act", act[prev]))
                    if c == 0 and pr < 2:
                        w.append(("at_act", act[("phik", NCH - 1, 6 + pr)]))
                    if c > 0 and pr < 2:
                        w.append(("at_dve", dv[("pqs", c - 1, 6 + pr)]))
                    p.add("tensor",
                          lambda e, o=bank[:],
                          l=rf_sb[:, pr * 128:(pr + 1) * 128],
                          r=at_qT[:, pr * 512:(pr + 1) * 512]:
                          e.matmul(o, l, r, start=True, stop=True),
                          waits=w, inc=("at_mm", 1),
                          tag=f"{phase}pq.{c}.{pr}")
                    mm[("phiq", c, pr)] = p.cnt["at_mm"]
                    wpq = [("at_mm", mm[("phiq", c, pr)])]
                    if c == 0:
                        wpq.append(("at_mm", kvmmf))
                    p.add("scalar",
                          lambda e, o=at_phiq[:, pr * 512:(pr + 1) * 512],
                          i=bank[:]: e.activation(o, i, AF.Relu),
                          waits=wpq,
                          inc=("at_act", 1), tag=f"{phase}pqe.{c}.{pr}")
                    act[("phiq", c, pr)] = p.cnt["at_act"]

                def emit_nrm(pr):
                    q = pr % 2
                    nrmbank = pt[5] if q == 0 else pt[1]
                    wn = []
                    if (c, pr) == (0, 0):
                        wn.append(("at_act", act["kvbd"]))
                        wn.append(("at_dve", dv["ksbd"]))
                    wn.append(("at_act", act[("phiq", c, pr)]))
                    prevn = ("nes", c, pr - 2) if pr >= 2 else (
                        ("nes", c - 1, PAIRS - 2 + pr) if c > 0 else None)
                    if prevn is not None and prevn in act and \
                            act[prevn] is not None:
                        wn.append(("at_act", act[prevn]))
                    if pr == 1:
                        wn.append(("at_act", act[("qt", c, NDT - 1)]))
                    p.add("tensor",
                          lambda e, o=nrmbank[0:2, :],
                          l=ksum_bd[:, pr * 2:(pr + 1) * 2],
                          r=at_phiq[:, pr * 512:(pr + 1) * 512]:
                          e.matmul(o, l, r, start=True, stop=True),
                          waits=wn, inc=("at_mm", 1), tag=f"{phase}n.{c}.{pr}")
                    mm[("nrm", c, pr)] = p.cnt["at_mm"]

                def emit_chain(pr):
                    q = pr % 2
                    nrmbank = pt[5] if q == 0 else pt[1]
                    rbmbank = pt[2 + q]
                    wne = [("at_mm", mm[("nrm", c, pr)])]
                    if ("rc", c, pr - 2) in act and \
                            act[("rc", c, pr - 2)] is not None:
                        wne.append(("at_act", act[("rc", c, pr - 2)]))
                    elif c > 0 and ("rc", c - 1, PAIRS - 2 + pr) in act:
                        wne.append(("at_act", act[("rc", c - 1,
                                                   PAIRS - 2 + pr)]))
                    p.add("scalar",
                          lambda e, o=nrm_sb[q], i=nrmbank[0:2, :]:
                          e.activation(o, i, AF.Ln, bias=np.float32(STAB)),
                          waits=wne, inc=("at_act", 1),
                          tag=f"{phase}ln.{c}.{pr}")
                    lnc = p.cnt["at_act"]
                    p.add("scalar",
                          lambda e, o=rec_sb[q], i=nrm_sb[q]:
                          e.activation(o, i, AF.Exp, scale=-1.0),
                          waits=[("at_act", lnc)], inc=("at_act", 1),
                          tag=f"{phase}ex.{c}.{pr}")
                    exc = p.cnt["at_act"]
                    dv[("nes", c, pr)] = None
                    act[("nes", c, pr)] = lnc
                    c1 = exc
                    dv[("rc", c, pr)] = None
                    act[("rc", c, pr)] = exc
                    wc2 = [("at_act", exc)]
                    if ("rbm", c, pr - 2) in mm:
                        wc2.append(("at_mm", mm[("rbm", c, pr - 2)]))
                    elif c > 0:
                        wc2.append(("at_mm", mm[("rbm", c - 1,
                                                 PAIRS - 2 + pr)]))
                    c2 = p.add("vector",
                               lambda e, o=rec_bf[q], i=rec_sb[q]:
                               e.tensor_copy(o, i),
                               waits=wc2, inc=("at_dve", 1),
                               tag=f"{phase}rb.{c}.{pr}")

                    wrb = [("at_dve", c2), K["bcmask"]]
                    if pr < 2:
                        wrb.append(("at_act", act[("phiq", c, 6 + pr)]))
                    else:
                        wrb.append(("at_dve", dv[("pqs", c, pr - 2)]))
                    p.add("tensor",
                          lambda e, o=rbmbank[:], rb=rec_bf[q]:
                          e.matmul(o, bcmask, rb, start=True, stop=True),
                          waits=wrb, inc=("at_mm", 1),
                          tag=f"{phase}rbm.{c}.{pr}")
                    mm[("rbm", c, pr)] = p.cnt["at_mm"]

                    p.add("vector",
                          lambda e, o=at_phiq[:, pr * 512:(pr + 1) * 512],
                          b=rbmbank[:]:
                          e.tensor_tensor(o, o, b, ALU.mult),
                          waits=[("at_mm", mm[("rbm", c, pr)])],
                          inc=("at_dve", 1), tag=f"{phase}pqs.{c}.{pr}")
                    dv[("pqs", c, pr)] = p.cnt["at_dve"]

                for pr in range(PAIRS + 1):
                    if pr < PAIRS:
                        emit_nrm(pr)
                    if pr >= 1:
                        emit_chain(pr - 1)

                for pr in range(PAIRS):
                    q = pr % 2
                    outbank = pt[4] if q == 0 else pt[0]
                    w = [("at_dve", dv[("pqs", c, pr)])]
                    if (c, pr) == (0, 0):
                        w.append(("at_act", act["kvbd"]))
                    prev = ("attnT", c, pr - 2) if pr >= 2 else (
                        ("attnT", c - 1, PAIRS - 2 + pr) if c > 0 else None)
                    if prev is not None and prev in dv:
                        w.append(("at_dve", dv[prev]))
                    if pr == 1:
                        w.append(("at_act", act[("qt", c, NDT - 2)]))
                    p.add("tensor",
                          lambda e, o=outbank[:],
                          l=kv_bd[:, pr * 128:(pr + 1) * 128],
                          r=at_phiq[:, pr * 512:(pr + 1) * 512]:
                          e.matmul(o, l, r, start=True, stop=True),
                          waits=w, inc=("at_mm", 1), tag=f"{phase}o.{c}.{pr}")
                    mm[("o", c, pr)] = p.cnt["at_mm"]
                    wat = [("at_mm", mm[("o", c, pr)])]
                    if c == 0 and pr < 2:
                        wat.append(("at_mm", kvmmf))
                    p.add("vector",
                          lambda e, o=at_attnT[:, pr * 512:(pr + 1) * 512],
                          a=outbank[:]:
                          e.tensor_copy(o, a),
                          waits=wat,
                          inc=("at_dve", 1), tag=f"{phase}at.{c}.{pr}")
                    dv[("attnT", c, pr)] = p.cnt["at_dve"]

                def emit_at_xin(nt):
                    w = [ffn1["xout_dma"][(E - 1, c, nt)]]
                    prev = prev_at_nt(c, nt)
                    if prev is not None:
                        w.append(("at_dve", dv[("wev", prev[0], prev[1], 1)]))
                    if c == 0 and nt < 2:
                        w.append(("evac", evf))
                        w.append(("ln_act", ln2f))
                    g = c * NT_CH + nt
                    asem = f"d_atxi{nt % 2}"
                    p.add("sync",
                          lambda e, d=at_xin[nt % 2],
                          s=xbuf[g * 128:(g + 1) * 128, :]:
                          e.dma_start(out=d, in_=s),
                          waits=w, inc=(asem, 16),
                          tag=f"{phase}xin.{c}.{nt}")
                    K[("at_xin", c, nt)] = (asem, p.cnt[asem])

                for nt in range(2):
                    emit_at_xin(nt)

                for nt in range(NT_CH):
                    for fh in range(2):
                        gi = nt * 2 + fh
                        bank = pt[6 + gi % 2]
                        w = [("at_dve", dv[("attnT", c, PAIRS - 1)]),
                             K["wout"]]
                        if c == 0 and gi < 2:
                            w.append(("at_act", act["kvbd"]))
                        prev = ("wo", c, gi - 2) if gi >= 2 else (
                            ("wo", c - 1, 6 + gi) if c > 0 else None)
                        if prev is not None:
                            pnt, pfh = prev[2] // 2, prev[2] % 2
                            w.append(("at_dve", dv[("wev", prev[1], pnt, pfh)]))
                        for jt in range(NDT):
                            p.add("tensor",
                                  lambda e, o=bank[:],
                                  l=at_attnT[:, jt * 512 + nt * 128:
                                             jt * 512 + (nt + 1) * 128],
                                  r=at_wout[:, jt * 1024 + fh * 512:
                                            jt * 1024 + fh * 512 + 512],
                                  st=(jt == 0):
                                  e.matmul(o, l, r, start=st, stop=False),
                                  waits=w if jt == 0 else [],
                                  tag=f"{phase}wo.{c}.{gi}.{jt}")
                        p.add("tensor",
                              lambda e, o=bank[:],
                              r=bout_row[0:1, fh * 512:fh * 512 + 512]:
                              e.matmul(o, ones_row, r, start=False, stop=True),
                              waits=[K["bout"]],
                              inc=("at_mm", 1), tag=f"{phase}wob.{c}.{gi}")
                        mm[("wo", c, gi)] = p.cnt["at_mm"]

                        wv = [("at_mm", mm[("wo", c, gi)]),
                              K[("at_xin", c, nt)]]
                        prevx = prev_at_nt(c, nt)
                        if prevx is not None:
                            wv.append(K[("at_xout", prevx[0], prevx[1])])
                        p.add("vector",
                              lambda e,
                              o=at_xout[nt % 2][:, fh * 512:fh * 512 + 512],
                              a=bank[:],
                              b=at_xin[nt % 2][:, fh * 512:fh * 512 + 512]:
                              e.tensor_tensor(o, a, b, ALU.add),
                              waits=wv, inc=("at_dve", 1),
                              tag=f"{phase}wev.{c}.{nt}.{fh}")
                        dv[("wev", c, nt, fh)] = p.cnt["at_dve"]

                    g = c * NT_CH + nt
                    osem = f"d_atxo{nt % 2}"
                    p.add("sync",
                          lambda e, d=xbuf[g * 128:(g + 1) * 128, :],
                          s=at_xout[nt % 2]: e.dma_start(out=d, in_=s),
                          waits=[("at_dve", dv[("wev", c, nt, 1)])],
                          inc=(osem, 16), tag=f"{phase}xo.{c}.{nt}")
                    K[("at_xout", c, nt)] = (osem, p.cnt[osem])
                    if nt + 2 < NT_CH:
                        emit_at_xin(nt + 2)

            return dict(mm=mm, act=act, dv=dv, ln2=ln2)

        def prev_at_nt(c, nt):
            if nt >= 2:
                return (c, nt - 2)
            if c > 0:
                return (c - 1, nt + NT_CH - 2)
            return None

        ln1 = emit_ln(x_in, gate=True, ut_dst="hbm", phase="L1.")

        if upto == "p1":
            if dbg_sel is not None:
                p.add("sync", lambda e: e.dma_start(out=dbg_sel, in_=sel_sb),
                      waits=[("gate_dve", K[("sel", "L1.")])],
                      inc=("d_cst_s", 16), tag="dbgsel")
            final_wait(p)
            p.verify()
            p.emit(nc, ctx)
            return nc

        ffn1 = emit_ffn(0, x_in, xbuf, xbuf, "L1.", ln1, phase="F1.")

        if upto == "ffn1":
            if dbg_sel is not None:
                p.add("sync", lambda e: e.dma_start(out=dbg_sel, in_=sel_sb),
                      waits=[("gate_dve", K[("sel", "L1.")])],
                      inc=("d_cst_s", 16), tag="dbgsel")
            final_wait(p)
            p.verify()
            p.emit(nc, ctx)
            return nc

        s2f = ffn1["s2_done"][(E - 1, NCH - 1, NT_CH - 1, 1)]
        evf = ffn1["evac_done"][(E - 1, NCH - 1, NT_CH - 1, 1)]

        def ln2_xwait(c, nt):
            return [ffn1["xout_dma"][(E - 1, c, nt)],
                    ("s2", s2f), ("evac", evf)]

        attn = emit_attn(ffn1, ln2_xwait, phase="A.")

        if upto == "attn":
            final_wait(p)
            p.verify()
            p.emit(nc, ctx)
            return nc

        at_mm_f = p.cnt["at_mm"]
        at_act_f = p.cnt["at_act"]
        at_dve_f = p.cnt["at_dve"]
        atxo_f = [("d_atxo0", p.cnt["d_atxo0"]), ("d_atxo1", p.cnt["d_atxo1"]),
                  ("d_atxi0", p.cnt["d_atxi0"]), ("d_atxi1", p.cnt["d_atxi1"])]

        def ln3_xwait(c, nt):
            return [K[("at_xout", c, nt)],
                    ("at_mm", at_mm_f), ("at_dve", at_dve_f),
                    ("at_act", at_act_f)] + atxo_f

        ln3 = emit_ln(xbuf, gate=True, ut_dst="hbm", xwait_fn=ln3_xwait,
                      phase="L3.")

        def ffn2_xsrc_wait(c, nt):
            return [K[("at_xout", c, nt)],
                    ("at_mm", at_mm_f), ("at_dve", at_dve_f),
                    ("at_act", at_act_f)] + atxo_f

        ffn2 = emit_ffn(1, xbuf, xbuf, y, "L3.", ln3,
                        wregion_waits=[("at_mm", at_mm_f)],
                        xtra_first_waits=[("at_dve", at_dve_f),
                                          ("at_act", at_act_f)],
                        xsrc_wait=ffn2_xsrc_wait, phase="F2.")

        final_wait(p)
        p.verify()
        p.emit(nc, ctx)
    return nc


def final_wait(p):
    waits = [(s, v) for s, v in p.cnt.items() if s.startswith("d_") and v > 0]
    p.add("sync", lambda e: None, waits=waits, tag="final")


def kernel(L0_attn_x, L0_attn_W_qkv, L0_attn_RF, L0_attn_W_out, L0_attn_b_out,
           L0_ffn_Wg, L0_ffn_bg, L0_ffn_W1, L0_ffn_b1, L0_ffn_W2, L0_ffn_b2):
    nc = build_program(debug_taps=DEBUG_TAPS, upto=UPTO)

    b1t = np.ascontiguousarray(
        np.asarray(L0_ffn_b1, dtype=np.float32).reshape(E, NHB, 128)
        .transpose(0, 2, 1))
    ident = np.eye(128, dtype=ml_dtypes.bfloat16)
    bcm = np.zeros((2, 128), dtype=ml_dtypes.bfloat16)
    bcm[0, 0:64] = 1
    bcm[1, 64:128] = 1
    qm = np.zeros((128, 16), dtype=ml_dtypes.bfloat16)
    qm.reshape(128, 8, 2)[0:64, :, 0] = 1
    qm.reshape(128, 8, 2)[64:128, :, 1] = 1

    common = dict(
        wqkv=np.asarray(L0_attn_W_qkv, dtype=np.float32),
        rf=np.asarray(L0_attn_RF, dtype=np.float32),
        wout=np.asarray(L0_attn_W_out, dtype=np.float32),
        bout=np.asarray(L0_attn_b_out, dtype=np.float32).reshape(1, D),
        wg=np.asarray(L0_ffn_Wg, dtype=np.float32),
        bg=np.asarray(L0_ffn_bg, dtype=np.float32).reshape(1, E),
        w1=np.asarray(L0_ffn_W1, dtype=np.float32),
        b1t=b1t,
        w2=np.asarray(L0_ffn_W2, dtype=np.float32),
        b2=np.asarray(L0_ffn_b2, dtype=np.float32),
        ident=ident,
        bcmask=bcm,
        qmask=qm,
    )
    xs = np.asarray(L0_attn_x, dtype=np.float32)
    in_maps = [dict(common, x=np.ascontiguousarray(xs[i])) for i in range(8)]
    res = run_bass_kernel_spmd(nc, in_maps, core_ids=list(range(8)))
    return np.stack([np.asarray(r["y"]) for r in res.results], axis=0)


# version 13
# speedup vs baseline: 1.1502x; 1.1502x over previous
import numpy as np
import ml_dtypes
from contextlib import ExitStack

import concourse.bass as bass
import concourse.mybir as mybir
from concourse.bass_utils import run_bass_kernel_spmd

F32 = mybir.dt.float32
BF16 = mybir.dt.bfloat16
AF = mybir.ActivationFunctionType
ALU = mybir.AluOpType
AX = mybir.AxisListType

ENGINES = ("tensor", "vector", "scalar", "gpsimd", "sync")

N = 4096
D = 1024
E = 8
H = 4096
NDT = D // 128
NHB = H // 128
CH = 512
NCH = N // CH
NT_CH = CH // 128
NNT = N // 128
HEADS = 16
PAIRS = 8
EPS = 1e-5
STAB = 1e-6

DEBUG_TAPS = ()
UPTO = "full"


class Prog:
    def __init__(self):
        self.ops = {e: [] for e in ENGINES}
        self.cnt = {}

    def sem(self, name):
        self.cnt.setdefault(name, 0)
        return name

    def add(self, eng, emit, waits=(), inc=None, tag=""):
        waits = [(s, v) for (s, v) in waits if v > 0]
        for s, v in waits:
            assert s in self.cnt, f"[{tag}] wait on unknown sem {s}"
            assert v <= self.cnt[s], (
                f"[{tag}] wait {s}>={v} references future (planned {self.cnt[s]})")
        post = None
        if inc is not None:
            s, amt = inc
            self.cnt.setdefault(s, 0)
            self.cnt[s] += amt
            post = self.cnt[s]
        self.ops[eng].append((emit, waits, inc, tag))
        return post

    def verify(self):
        idx = {e: 0 for e in ENGINES}
        val = {s: 0 for s in self.cnt}
        progress = True
        while progress:
            progress = False
            for e in ENGINES:
                while idx[e] < len(self.ops[e]):
                    emit, waits, inc, tag = self.ops[e][idx[e]]
                    if any(val[s] < v for s, v in waits):
                        break
                    if inc is not None:
                        val[inc[0]] += inc[1]
                    idx[e] += 1
                    progress = True
        stuck = {e: idx[e] for e in ENGINES if idx[e] < len(self.ops[e])}
        if stuck:
            lines = []
            for e, i in stuck.items():
                emit, waits, inc, tag = self.ops[e][i]
                missing = [(s, v, val[s]) for s, v in waits if val[s] < v]
                lines.append(f"  {e}[{i}] tag={tag} missing={missing}")
            raise RuntimeError("DEADLOCK in schedule:\n" + "\n".join(lines))
        for s, v in self.cnt.items():
            assert val[s] == v, f"sem {s} ended {val[s]} != planned {v}"

    def emit(self, nc, ctx):
        handles = {s: ctx.enter_context(nc.semaphore(s)) for s in self.cnt}
        block = ctx.enter_context(nc.Block())

        def runner(ename):
            def run(eng):
                for emit, waits, inc, tag in self.ops[ename]:
                    for s, v in waits:
                        eng.wait_ge(handles[s], v)
                    inst = emit(eng)
                    if inc is not None:
                        assert inst is not None, f"inc w/o instruction: {tag}"
                        inst.then_inc(handles[inc[0]], inc[1])
            return run

        block.tensor(runner("tensor"))
        block.vector(runner("vector"))
        block.scalar(runner("scalar"))
        block.gpsimd(runner("gpsimd"))
        block.sync(runner("sync"))


def build_program(debug_taps=(), upto="full"):
    nc = bass.Bass("TRN2", target_bir_lowering=False, debug=False)
    p = Prog()
    for s in ("gp_c", "s1", "s2", "gelu", "evac", "w2f", "corr", "ev1",
              "ln_act", "ln_dve", "ln_mm", "gate_mm", "gate_dve",
              "at_mm", "at_act", "at_dve"):
        p.sem(s)
    DMA_SEMS = (["d_cst_s", "d_cst_g", "d_rf", "d_wqkv", "d_wout"]
                + [f"d_w1q{i}" for i in range(8)]
                + [f"d_w2q{i}" for i in range(8)]
                + ["d_b2_0", "d_b2_1"]
                + [f"d_xst{i}" for i in range(4)]
                + ["d_utw0", "d_utw1", "d_utw2", "d_utw3",
                   "d_utb0", "d_utb1",
                   "d_xin0", "d_xin1", "d_xout0", "d_xout1",
                   "d_atxi0", "d_atxi1", "d_atxo0", "d_atxo1"])
    for s in DMA_SEMS:
        p.sem(s)
    K = {}

    def din(name, shape, dtype=F32):
        return nc.dram_tensor(name, list(shape), dtype, kind="ExternalInput").ap()

    x_in = din("x", (N, D))
    wqkv_in = din("wqkv", (D, 3 * D))
    rf_in = din("rf", (HEADS, 64, 64))
    wout_in = din("wout", (D, D))
    bout_in = din("bout", (1, D))
    wg_in = din("wg", (D, E))
    bg_in = din("bg", (1, E))
    w1_in = din("w1", (E, D, H))
    b1t_in = din("b1t", (E, 128, NHB))
    w2_in = din("w2", (E, H, D))
    b2_in = din("b2", (E, D))
    ident_in = din("ident", (128, 128), BF16)
    bcmask_in = din("bcmask", (2, 128), BF16)
    qmask_in = din("qmask", (128, 16), BF16)

    y = nc.dram_tensor("y", [N, D], F32, kind="ExternalOutput").ap()
    xbuf_kind = ("ExternalOutput"
                 if ("xbuf_ffn1" in debug_taps or "xbuf_attn" in debug_taps)
                 else "Internal")
    xbuf = nc.dram_tensor("xbuf", [N, D], F32, kind=xbuf_kind).ap()
    uT_kind = "ExternalOutput" if "uT" in debug_taps else "Internal"
    uT_hbm = nc.dram_tensor("uT_hbm", [128, NNT * 1024], BF16, kind=uT_kind).ap()
    dbg_sel = (nc.dram_tensor("dbg_sel", [128, NNT * E], F32,
                              kind="ExternalOutput").ap()
               if "sel" in debug_taps else None)

    ARENA = 212352

    cur = [0]

    def nxt(nbytes, align=32):
        cur[0] = (cur[0] + align - 1) // align * align
        off = cur[0]
        cur[0] += nbytes
        return off

    PO_IDENT = nxt(256)
    PO_ONESR = nxt(256)
    PO_ONESC = nxt(32)
    PO_BCMASK = nxt(256)
    PO_RF = nxt(2048)
    PO_WG = nxt(128)
    PO_BGROW = nxt(32)
    PO_B1COL = nxt(1024)
    PO_SEL = nxt(1024)
    PO_BOUTR = nxt(2048)
    PO_KSBD = nxt(32)
    PO_KVBD = nxt(2048)
    PO_QM = nxt(32)
    PO_SELBF = nxt(512)
    PO_SELT = nxt(1024)
    PO_B2R = nxt(2048)
    BIG = nxt(0, 64)
    assert BIG <= 16 * 1024, BIG

    FO_W1 = BIG
    FO_W2 = FO_W1 + 65536
    FO_HT = FO_W2 + 65536
    FO_UT = FO_HT + 32768
    FO_XIN = FO_UT + 16384
    FO_XOUT = FO_XIN + 8192
    FO_END = FO_XOUT + 8192
    assert FO_END <= ARENA, FO_END

    LO = FO_HT
    LO_XST = LO
    LO_SQB = LO_XST + 16384
    LO_UBF = LO_SQB + 4096
    LO_UTS = LO_UBF + 8192
    LO_GT = LO_UTS + 8192
    LO_GS = LO_GT + 4096
    LO_STAT = LO_GS + 384
    LO_END = LO_STAT + 768
    assert LO_END <= FO_XIN

    AO_UT = BIG
    AO_WQKV = AO_UT + 65536
    AO_WOUT = AO_WQKV + 49152
    AO_P1 = AO_WOUT + 16384
    AO_KT = (LO_END + FO_HT - LO + 31) // 32 * 32
    AO_V = AO_KT + 8192
    AO_PHIK = AO_V + 8192
    assert AO_PHIK + 8192 <= ARENA, AO_PHIK
    AO_QT = AO_P1
    AO_PHIQ = AO_QT + 8192
    AO_ATTNT = AO_PHIQ + 8192
    AO_RB = AO_ATTNT + 8192
    AO_XIN = AO_RB + 2048
    AO_XOUT = AO_XIN + 8192
    AO_NRM = AO_XOUT + 8192
    AO_NRM2 = AO_NRM + 2048
    AO_REC = AO_NRM2 + 2048
    AO_REC2 = AO_REC + 2048
    AO_RECBF = AO_REC2 + 2048
    AO_RECBF2 = AO_RECBF + 1024
    AO_RB2 = AO_RECBF2 + 1024
    assert AO_RB2 + 2048 <= AO_PHIK + 8192
    AO_KVACC = LO_GT
    assert AO_KVACC + 4224 <= LO_STAT, (AO_KVACC, LO_STAT)
    assert AO_PHIK + 8192 <= ARENA, AO_PHIK

    with ExitStack() as ctx:
        arena = ctx.enter_context(
            nc.sbuf_tensor("arena", [128, ARENA], mybir.dt.uint8))

        def carve(off, part, cols, dtype):
            nb = cols * mybir.dt.size(dtype)
            assert off + nb <= ARENA
            return arena[0:part, off:off + nb].bitcast(dtype)

        pt = [ctx.enter_context(nc.psum_tensor(f"pt{i}", [128, 512], F32))
              for i in range(8)]

        ident = carve(PO_IDENT, 128, 128, BF16)
        ones_row = carve(PO_ONESR, 1, 128, BF16)
        ones_col = carve(PO_ONESC, 128, 1, BF16)
        bcmask = carve(PO_BCMASK, 2, 128, BF16)
        rf_sb = carve(PO_RF, 128, 1024, BF16)
        wg_sb = carve(PO_WG, 128, 64, BF16)
        bg_row = carve(PO_BGROW, 1, 8, BF16)
        b1col = carve(PO_B1COL, 128, 256, F32)
        sel_sb = carve(PO_SEL, 128, 256, F32)
        bout_row = carve(PO_BOUTR, 1, 1024, BF16)
        ksum_bd = carve(PO_KSBD, 128, 16, BF16)
        kv_bd = carve(PO_KVBD, 128, 1024, BF16)
        ksum_qm = carve(PO_QM, 128, 16, BF16)
        selbf = carve(PO_SELBF, 128, 256, BF16)
        selT = carve(PO_SELT, 8, 512, BF16)
        b2r = carve(PO_B2R, 8, 1024, BF16)

        w1sb = carve(FO_W1, 128, 32 * 1024, BF16)
        w2sb = carve(FO_W2, 128, 32 * 1024, BF16)
        hT = carve(FO_HT, 128, 32 * 512, BF16)
        uTb = [carve(FO_UT + i * 8192, 128, 8 * 512, BF16) for i in range(2)]
        xinb = [carve(FO_XIN + i * 4096, 128, 1024, F32) for i in range(2)]
        xoutb = [carve(FO_XOUT + i * 4096, 128, 1024, F32) for i in range(2)]

        xst = [carve(LO_XST + i * 4096, 128, 1024, F32) for i in range(4)]
        ubf = [carve(LO_UBF + i * 2048, 128, 1024, BF16) for i in range(4)]
        uts = carve(LO_UTS, 128, 4096, BF16)
        gt = [carve(LO_GT + i * 1024, 128, 256, F32) for i in range(4)]
        gs = carve(LO_GS, 128, 96, F32)
        stat = carve(LO_STAT, 128, 192, F32)

        at_uT = carve(AO_UT, 128, 8 * 4096, BF16)
        at_wqkv = carve(AO_WQKV, 128, 8 * 3072, BF16)
        at_wout = carve(AO_WOUT, 128, 8 * 1024, BF16)
        at_kT = carve(AO_KT, 128, 8 * 512, BF16)
        at_v = carve(AO_V, 128, 4 * 1024, BF16)
        at_phik = carve(AO_PHIK, 128, 4 * 1024, BF16)
        at_qT = carve(AO_QT, 128, 8 * 512, BF16)
        at_phiq = carve(AO_PHIQ, 128, 8 * 512, BF16)
        at_attnT = carve(AO_ATTNT, 128, 8 * 512, BF16)
        at_rb = carve(AO_RB, 128, 512, F32)
        at_xin = [carve(AO_XIN + i * 4096, 128, 1024, F32) for i in range(2)]
        at_xout = [carve(AO_XOUT + i * 4096, 128, 1024, F32) for i in range(2)]
        nrm_sb = [carve(AO_NRM, 2, 512, F32), carve(AO_NRM2, 2, 512, F32)]
        rec_sb = [carve(AO_REC, 2, 512, F32), carve(AO_REC2, 2, 512, F32)]
        rec_bf = [carve(AO_RECBF, 2, 512, BF16),
                  carve(AO_RECBF2, 2, 512, BF16)]
        kv_acc = carve(AO_KVACC, 128, 8 * 132, F32)
        at_rb2 = carve(AO_RB2, 128, 512, F32)

        p.add("sync", lambda e: e.dma_start(out=ident, in_=ident_in),
              inc=("d_cst_s", 16), tag="ld_ident")
        K["ident"] = ("d_cst_s", p.cnt["d_cst_s"])
        p.add("sync", lambda e: e.dma_start(out=bcmask, in_=bcmask_in),
              waits=[("d_cst_s", 16)], inc=("d_cst_s", 16), tag="ld_bcmask")
        K["bcmask"] = ("d_cst_s", p.cnt["d_cst_s"])
        p.add("sync", lambda e: e.dma_start(out=ksum_qm, in_=qmask_in),
              waits=[("d_cst_s", 32)], inc=("d_cst_s", 16), tag="ld_qmask")
        K["qmask"] = ("d_cst_s", p.cnt["d_cst_s"])

        p.add("gpsimd", lambda e: e.memset(ones_row, 1.0), inc=("gp_c", 1),
              tag="ones_row")
        p.add("gpsimd", lambda e: e.memset(ones_col, 1.0), inc=("gp_c", 1),
              tag="ones_col")
        p.add("gpsimd", lambda e: e.memset(rf_sb, 0.0), inc=("gp_c", 1),
              tag="rf_zero")
        p.add("gpsimd", lambda e: e.memset(kv_bd, 0.0), inc=("gp_c", 1),
              tag="kvbd_zero")
        p.add("gpsimd", lambda e: e.memset(ksum_bd, 0.0), inc=("gp_c", 1),
              tag="ksbd_zero")
        K["ones"] = p.cnt["gp_c"]
        p.add("gpsimd", lambda e: e.dma_start(
            out=wg_sb.rearrange("p (dt k) -> p dt k", k=E),
            in_=wg_in.rearrange("(dt p) k -> p dt k", p=128)),
            inc=("d_cst_g", 16), tag="wg")
        K["wg"] = ("d_cst_g", p.cnt["d_cst_g"])
        p.add("gpsimd", lambda e: e.dma_start(out=bg_row, in_=bg_in),
              waits=[("d_cst_g", 16)], inc=("d_cst_g", 16), tag="bg")
        K["bg"] = ("d_cst_g", p.cnt["d_cst_g"])
        for h in range(HEADS):
            pr, q = h // 2, h % 2
            dst = rf_sb[64 * q:64 * (q + 1),
                        pr * 128 + 64 * q:pr * 128 + 64 * (q + 1)]
            p.add("gpsimd", lambda e, d=dst, s=rf_in[h]: e.dma_start(out=d, in_=s),
                  waits=[("gp_c", K["ones"]), ("d_rf", 16 * h)],
                  inc=("d_rf", 16), tag=f"rf{h}")
        K["rf"] = ("d_rf", p.cnt["d_rf"])
        p.add("gpsimd", lambda e: e.dma_start(out=bout_row, in_=bout_in),
              waits=[("d_cst_g", 32)], inc=("d_cst_g", 16), tag="bout")
        K["bout"] = ("d_cst_g", p.cnt["d_cst_g"])
        p.add("gpsimd", lambda e: e.dma_start(
            out=b1col.rearrange("p (k hb) -> p k hb", k=E),
            in_=b1t_in.rearrange("k p hb -> p k hb")),
            waits=[("d_cst_g", 48)], inc=("d_cst_g", 16), tag="b1col")
        K["b1col"] = ("d_cst_g", p.cnt["d_cst_g"])
        p.add("gpsimd", lambda e: e.dma_start(out=b2r, in_=b2_in),
              waits=[("d_cst_g", 64)], inc=("d_cst_g", 16), tag="b2r")
        K["b2r"] = ("d_cst_g", p.cnt["d_cst_g"])

        def load_w1q(call, e_, q, waits, tag):
            dst = w1sb[:, q * 4096:(q + 1) * 4096].rearrange(
                "p (dt k) -> p dt k", dt=NDT)
            src = w1_in[e_].rearrange("(dt p) hh -> p dt hh",
                                      p=128)[:, :, q * 512:(q + 1) * 512]
            sem = f"d_w1q{q}"
            cnt = p.add(
                "gpsimd", lambda e, d=dst, s=src: e.dma_start(out=d, in_=s),
                waits=waits, inc=(sem, 16), tag=tag)
            K[("w1", call, e_, q)] = (sem, cnt)

        def load_w2q(call, e_, q, waits, tag):
            dst = w2sb[:, q * 4096:(q + 1) * 4096].rearrange(
                "p (hb f) -> p hb f", hb=4)
            src = w2_in[e_, q * 512:(q + 1) * 512, :].rearrange(
                "(hb pp) f -> pp hb f", pp=128)
            sem = f"d_w2q{q}"
            cnt = p.add(
                "gpsimd", lambda e, d=dst, s=src: e.dma_start(out=d, in_=s),
                waits=waits, inc=(sem, 16), tag=tag)
            K[("w2", call, e_, q)] = (sem, cnt)

        ln_idx = [0]

        def emit_ln(src_hbm, gate, ut_dst, xwait_fn=None, phase="",
                    chunk_cb=None):
            ph = ln_idx[0]
            ln_idx[0] += 1
            norm_done, evac_done, tp_done = {}, {}, {}
            bn_done, pack_done, aggr_done, recip_done, xdma = {}, {}, {}, {}, {}
            gmm_done, sqrt_done = {}, {}
            utw = {}
            tpb = [4, 5] if ut_dst == "sbuf" else [2, 3, 4, 5]
            ntpb = len(tpb)

            for c in range(NCH):
                for nt in range(NT_CH):
                    g = c * NT_CH + nt
                    sl = (g % 4) * 48
                    w = []
                    if g >= 4:
                        w.append(("ln_dve", norm_done[g - 4]))
                    if xwait_fn is not None:
                        w += xwait_fn(c, nt)
                    src = src_hbm[g * 128:(g + 1) * 128, :]
                    xsem = f"d_xst{g % 4}"
                    p.add("sync",
                          lambda e, d=xst[g % 4], s=src: e.dma_start(out=d, in_=s),
                          waits=w, inc=(xsem, 16), tag=f"{phase}xld{g}")
                    xdma[g] = (xsem, p.cnt[xsem])

                    ws = [xdma[g]]
                    if g >= 4:
                        ws.append(("ln_dve", pack_done[g - 4]))
                    p.add("vector",
                          lambda e, o=stat[:, sl:sl + 6],
                          i=xst[g % 4][:, 0:512]:
                          e.bn_stats(o, i),
                          waits=ws, inc=("ln_dve", 1), tag=f"{phase}bna{g}")
                    p.add("vector",
                          lambda e, o=stat[:, sl + 8:sl + 14],
                          i=xst[g % 4][:, 512:1024]:
                          e.bn_stats(o, i),
                          waits=[xdma[g]], inc=("ln_dve", 1),
                          tag=f"{phase}bnb{g}")
                    bn_done[g] = p.cnt["ln_dve"]

                    wp = [("ln_dve", bn_done[g])]
                    if g >= 4:
                        wp.append(("ln_dve", aggr_done[g - 4]))
                    p.add("vector",
                          lambda e,
                          o=stat[:, sl + 16:sl + 28].rearrange(
                              "p (r k) -> p r k", k=6),
                          i=stat[:, sl:sl + 16].rearrange(
                              "p (r k) -> p r k", k=8)[:, :, 0:6]:
                          e.tensor_copy(o, i),
                          waits=wp, inc=("ln_dve", 1), tag=f"{phase}pk{g}")
                    pack_done[g] = p.cnt["ln_dve"]

                    wa = [("ln_dve", pack_done[g])]
                    if g >= 4:
                        wa.append(("ln_act", sqrt_done[g - 4]))
                        wa.append(("ln_dve", norm_done[g - 4]))
                    p.add("vector",
                          lambda e, o=stat[:, sl + 32:sl + 34],
                          i=stat[:, sl + 16:sl + 28]:
                          e.bn_aggr(o, i),
                          waits=wa, inc=("ln_dve", 1), tag=f"{phase}agg{g}")
                    aggr_done[g] = p.cnt["ln_dve"]

                    wsq = [("ln_dve", aggr_done[g])]
                    if g >= 4:
                        wsq.append(("ln_dve", recip_done[g - 4]))
                    p.add("scalar",
                          lambda e, o=stat[:, sl + 36:sl + 37],
                          i=stat[:, sl + 33:sl + 34]:
                          e.activation(o, i, AF.Sqrt, bias=np.float32(EPS)),
                          waits=wsq, inc=("ln_act", 1), tag=f"{phase}sqrt{g}")
                    sqrt_done[g] = p.cnt["ln_act"]

                    wr = [("ln_act", sqrt_done[g])]
                    if g >= 4:
                        wr.append(("ln_dve", norm_done[g - 4]))
                    p.add("vector",
                          lambda e, o=stat[:, sl + 40:sl + 41],
                          i=stat[:, sl + 36:sl + 37]:
                          e.reciprocal(o, i),
                          waits=wr, inc=("ln_dve", 1), tag=f"{phase}rcp{g}")
                    recip_done[g] = p.cnt["ln_dve"]

                    w = [("ln_dve", recip_done[g])]
                    if g >= 4:
                        w.append(("ln_mm", tp_done[g - 4]))
                    p.add("vector",
                          lambda e, o=ubf[g % 4], i=xst[g % 4],
                          mn=stat[:, sl + 32:sl + 33],
                          rs=stat[:, sl + 40:sl + 41]:
                          e.tensor_scalar(o, i, mn, rs, ALU.subtract, ALU.mult),
                          waits=w, inc=("ln_dve", 1), tag=f"{phase}nrm{g}")
                    norm_done[g] = p.cnt["ln_dve"]

                    bank = pt[tpb[g % ntpb]][:].bitcast(BF16)
                    for dt in range(NDT):
                        w = [("ln_dve", norm_done[g])] if dt == 0 else []
                        if dt == 0:
                            w.append(K["ident"])
                            if g >= ntpb:
                                w.append(("ln_act", evac_done[g - ntpb]))
                        p.add("tensor",
                              lambda e, o=bank[:, dt * 128:(dt + 1) * 128],
                              i=ubf[g % 4][:, dt * 128:(dt + 1) * 128]:
                              e.transpose(o, i, ident),
                              waits=w,
                              inc=("ln_mm", 1) if dt == NDT - 1 else None,
                              tag=f"{phase}tp{g}.{dt}")
                    tp_done[g] = p.cnt["ln_mm"]

                    def emit_tileB(gB):
                        cB, ntB = divmod(gB, NT_CH)
                        bankB = pt[tpb[gB % ntpb]][:].bitcast(BF16)
                        if ut_dst == "hbm":
                            dsts = uts.rearrange(
                                "p (dt t) -> p dt t", dt=NDT)[
                                :, :, ntB * 128:(ntB + 1) * 128]
                        else:
                            dsts = at_uT.rearrange(
                                "p (dt n) -> p dt n", dt=NDT)[
                                :, :, gB * 128:(gB + 1) * 128]
                        w = [("ln_mm", tp_done[gB])]
                        if ut_dst == "hbm" and cB >= 1:
                            w.append(utw[cB - 1])
                            if gate:
                                w.append(("gate_mm",
                                          gmm_done[(cB - 1) * NT_CH + ntB]))
                        p.add("scalar",
                              lambda e, o=dsts,
                              i=bankB.rearrange("p (dt k) -> p dt k", dt=NDT):
                              e.activation(o, i, AF.Identity),
                              waits=w, inc=("ln_act", 1), tag=f"{phase}ev{gB}")
                        evac_done[gB] = p.cnt["ln_act"]

                        if ut_dst == "hbm" and ntB == NT_CH - 1:
                            usem = f"d_utw{cB % 4}"
                            p.add("scalar",
                                  lambda e, s=uts,
                                  d=uT_hbm[:, cB * 4096:(cB + 1) * 4096]:
                                  e.dma_start(out=d, in_=s),
                                  waits=[("ln_act", evac_done[gB])],
                                  inc=(usem, 16), tag=f"{phase}ud{cB}")
                            utw[cB] = (usem, p.cnt[usem])
                            K[("ut", ph, cB)] = utw[cB]

                        if gate:
                            for dt in range(NDT):
                                w = []
                                if dt == 0:
                                    w = [("ln_act", evac_done[gB]),
                                         K["wg"], K["bg"],
                                         ("gp_c", K["ones"])]
                                p.add("tensor",
                                      lambda e,
                                      o=pt[6][:, gB * 8:(gB + 1) * 8],
                                      l=uts[:, dt * 512 + ntB * 128:
                                            dt * 512 + (ntB + 1) * 128],
                                      r=wg_sb[:, dt * 8:(dt + 1) * 8],
                                      st=(dt == 0):
                                      e.matmul(o, l, r, start=st, stop=False),
                                      waits=w, tag=f"{phase}g{gB}.{dt}")
                            p.add("tensor",
                                  lambda e, o=pt[6][:, gB * 8:(gB + 1) * 8]:
                                  e.matmul(o, ones_row, bg_row, start=False,
                                           stop=True),
                                  inc=("gate_mm", 1), tag=f"{phase}gb{gB}")
                            gmm_done[gB] = p.cnt["gate_mm"]

                    if ut_dst == "hbm":
                        if g >= 1:
                            emit_tileB(g - 1)
                        if g == NNT - 1:
                            emit_tileB(g)
                    else:
                        emit_tileB(g)

                if chunk_cb is not None:
                    chunk_cb(c, dict(evac_done=evac_done, norm_done=norm_done,
                                     tp_done=tp_done))

            if gate:
                emit_gate_softmax(phase, gmm_done[NNT - 1])
            if ut_dst == "hbm":
                K[("utw_final", ph)] = [(f"d_utw{i}", p.cnt[f"d_utw{i}"])
                                        for i in range(4)]
            return dict(ph=ph, norm_done=norm_done, evac_done=evac_done,
                        xdma=xdma)

        def emit_gate_softmax(phase, mm_cnt):
            v3 = lambda ap: ap.rearrange("p (g k) -> p g k", k=E)
            S0 = lambda: gs[:, 0:32]
            S1 = lambda: gs[:, 32:64]

            def dve(fn, waits=(), tag=""):
                p.add("vector", fn, waits=waits, inc=("gate_dve", 1),
                      tag=phase + tag)
                return p.cnt["gate_dve"]

            g3 = lambda col: col.rearrange("p (g o) -> p g o", o=1)
            bc = lambda col: g3(col).broadcast_to([128, 32, E])

            mxc = gs[:, 64:96]
            c1 = dve(lambda e: e.reduce_max(g3(mxc), v3(pt[6][:, 0:256]),
                                            axis=AX.X),
                     waits=[("gate_mm", mm_cnt)], tag="gmx")
            c2 = dve(lambda e: e.tensor_tensor(v3(gt[0]), v3(pt[6][:, 0:256]),
                                               bc(mxc), ALU.subtract),
                     waits=[("gate_dve", c1)], tag="gxm")
            p.add("scalar", lambda e: e.activation(gt[1], gt[0], AF.Exp),
                  waits=[("gate_dve", c2)], inc=("ln_act", 1),
                  tag=phase + "gexp")
            ce = p.cnt["ln_act"]
            c4 = dve(lambda e: e.reduce_sum(g3(S0()), v3(gt[1]), axis=AX.X),
                     waits=[("ln_act", ce)], tag="gss")
            c5 = dve(lambda e: e.reciprocal(S1(), S0()),
                     waits=[("gate_dve", c4)], tag="grs")
            c6 = dve(lambda e: e.tensor_tensor(v3(gt[2]), v3(gt[1]), bc(S1()),
                                               ALU.mult),
                     waits=[("gate_dve", c5)], tag="gpr")
            c7 = dve(lambda e: e.reduce_max(g3(S0()), v3(gt[2]), axis=AX.X),
                     waits=[("gate_dve", c6)], tag="gm1")
            c8 = dve(lambda e: e.tensor_tensor(v3(gt[0]), v3(gt[2]), bc(S0()),
                                               ALU.is_ge),
                     waits=[("gate_dve", c7)], tag="gmk1")
            c9 = dve(lambda e: e.reduce_sum(g3(S1()), v3(gt[0]), axis=AX.X),
                     waits=[("gate_dve", c8)], tag="gc1")
            c10 = dve(lambda e: e.scalar_tensor_tensor(
                gt[1], gt[0], -1.0e9, gt[2], ALU.mult, ALU.add),
                waits=[("gate_dve", c9)], tag="gpm")
            c11 = dve(lambda e: e.reduce_max(g3(mxc), v3(gt[1]), axis=AX.X),
                      waits=[("gate_dve", c10)], tag="gm2")
            c12 = dve(lambda e: e.tensor_tensor(v3(gt[3]), v3(gt[1]), bc(mxc),
                                                ALU.is_ge),
                      waits=[("gate_dve", c11)], tag="gmk2")
            c13 = dve(lambda e: e.tensor_scalar(mxc, S1(), 1.5, None,
                                                ALU.is_le),
                      waits=[("gate_dve", c12)], tag="gn2")
            c14a = dve(lambda e: e.tensor_tensor(v3(gt[3]), v3(gt[3]), bc(mxc),
                                                 ALU.mult),
                       waits=[("gate_dve", c13)], tag="gmka")
            c14 = dve(lambda e: e.tensor_tensor(gt[1], gt[3], gt[0], ALU.add),
                      waits=[("gate_dve", c14a)], tag="gmkb")
            c15 = dve(lambda e: e.tensor_tensor(gt[0], gt[2], gt[1], ALU.mult),
                      waits=[("gate_dve", c14)], tag="gsr")
            c16 = dve(lambda e: e.reduce_sum(g3(S0()), v3(gt[0]), axis=AX.X),
                      waits=[("gate_dve", c15)], tag="gdn")
            c17 = dve(lambda e: e.tensor_scalar(S1(), S0(), 1e-8, None,
                                                ALU.add),
                      waits=[("gate_dve", c16)], tag="gdp")
            c18 = dve(lambda e: e.reciprocal(S0(), S1()),
                      waits=[("gate_dve", c17)], tag="grd")
            c19 = dve(lambda e: e.tensor_scalar(S1(), S0(), 0.5, None,
                                                ALU.mult),
                      waits=[("gate_dve", c18)], tag="grd5")
            c20 = dve(lambda e: e.tensor_tensor(v3(sel_sb), v3(gt[0]), bc(S1()),
                                                ALU.mult),
                      waits=[("gate_dve", c19)], tag="gsel")
            K[("sel", phase)] = c20
            c21 = dve(lambda e: e.tensor_copy(selbf, sel_sb),
                      waits=[("gate_dve", c20)], tag="gselbf")
            K[("selbf", phase)] = c21

        def emit_ffn(call, xsrc_hbm, xdst_hbm, xfin_hbm, selphase, ln_info,
                     wregion_waits=(), xtra_first_waits=(), xsrc_wait=None,
                     phase=""):
            s1_done, gelu_done, s2_done, evac_done = {}, {}, {}, {}
            xout_dma, xin_dma, ut_dma = {}, {}, {}
            w2f_base = p.cnt["w2f"]
            lnph = ln_info["ph"]

            for q in range(8):
                load_w1q(call, 0, q, list(wregion_waits) if q == 0 else [],
                         f"{phase}w1.0.{q}")
            for q in range(8):
                load_w2q(call, 0, q, [], f"{phase}w2.0.{q}")

            def emit_utload(cj):
                cc = cj % NCH
                w = [K[("ut", lnph, cc)]]
                if cj >= 2:
                    w.append(("s1", s1_done[(cj - 2) * NHB + NHB - 1]))
                if cj <= 1:
                    w += list(xtra_first_waits)
                    w.append(("gate_dve", K[("sel", selphase)]))
                    w += K[("utw_final", lnph)]
                usem = f"d_utb{cc % 2}"
                p.add("sync",
                      lambda e, d=uTb[cc % 2],
                      s=uT_hbm[:, cc * 4096:(cc + 1) * 4096]:
                      e.dma_start(out=d, in_=s),
                      waits=w, inc=(usem, 16), tag=f"{phase}utd{cj}")
                ut_dma[cj] = (usem, p.cnt[usem])

            emit_utload(0)
            emit_utload(1)
            for e_ in range(E):
                for c in range(NCH):
                    ci = e_ * NCH + c

                    def emit_xin(nt):
                        g = c * NT_CH + nt
                        w = []
                        if e_ > 0:
                            w.append(xout_dma[(e_ - 1, c, nt)])
                        elif xsrc_wait is not None:
                            w += xsrc_wait(c, nt)
                        prev = prev_nt(e_, c, nt)
                        if prev is not None:
                            w.append(("evac", evac_done[(prev[0], prev[1],
                                                         prev[2], 1)]))
                        if (e_, c, nt) == (0, 0, 0):
                            w += list(xtra_first_waits)
                        src = (xsrc_hbm if e_ == 0 else xdst_hbm)[
                            g * 128:(g + 1) * 128, :]
                        xs = f"d_xin{nt % 2}"
                        p.add("sync",
                              lambda e, d=xinb[nt % 2], s=src:
                              e.dma_start(out=d, in_=s),
                              waits=w, inc=(xs, 16),
                              tag=f"{phase}xin{e_}.{c}.{nt}")
                        xin_dma[(e_, c, nt)] = (xs, p.cnt[xs])

                    for nt in range(2):
                        emit_xin(nt)

                    for hb in range(NHB):
                        gi = ci * NHB + hb
                        w = []
                        if hb == 0:
                            w.append(ut_dma[ci])
                        if hb % 4 == 0:
                            w.append(K[("w1", call, e_, hb // 4)])
                        if gi >= 2:
                            w.append(("gelu", gelu_done[gi - 2]))
                        w1base = (hb // 4) * 4096 + (hb % 4) * 128
                        for dt in range(NDT):
                            p.add("tensor",
                                  lambda e, o=pt[hb % 2][:],
                                  l=w1sb[:, w1base + dt * 512:
                                         w1base + dt * 512 + 128],
                                  r=uTb[c % 2][:, dt * CH:(dt + 1) * CH],
                                  st=(dt == 0), sp=(dt == NDT - 1):
                                  e.matmul(o, l, r, start=st, stop=sp),
                                  waits=w if dt == 0 else [],
                                  inc=("s1", 1) if dt == NDT - 1 else None,
                                  tag=f"{phase}s1.{e_}.{c}.{hb}.{dt}")
                        s1_done[gi] = p.cnt["s1"]

                        p.add("scalar",
                              lambda e, o=hT[:, hb * CH:(hb + 1) * CH],
                              i=pt[hb % 2][:],
                              b=b1col[:, e_ * NHB + hb:e_ * NHB + hb + 1]:
                              e.activation(o, i, AF.Gelu_apprx_tanh, bias=b),
                              waits=[("s1", s1_done[gi]), K["b1col"]],
                              inc=("gelu", 1), tag=f"{phase}gl.{e_}.{c}.{hb}")
                        gelu_done[gi] = p.cnt["gelu"]

                    if ci + 2 < E * NCH:
                        emit_utload(ci + 2)

                    lastchunk = (c == NCH - 1)
                    pass_idx = ci * 4
                    if e_ == E - 1:
                        ps0bf = pt[0][:].bitcast(BF16)
                        for nt in range(NT_CH):
                            w = [("gelu", gelu_done[ci * NHB + 30])]
                            if nt == 0:
                                w.append(("gate_dve",
                                          K[("selbf", selphase)]))
                                if ("corr_done", c - 1) in K:
                                    w.append(K[("corr_done", c - 1)])
                            p.add("tensor",
                                  lambda e,
                                  o=ps0bf[0:8, nt * 128:(nt + 1) * 128],
                                  i=selbf[:, (c * NT_CH + nt) * 8:
                                          (c * NT_CH + nt + 1) * 8]:
                                  e.transpose(o, i, ident),
                                  waits=w,
                                  inc=("ln_mm", 1) if nt == NT_CH - 1
                                  else None,
                                  tag=f"{phase}selt.{c}.{nt}")
                        seltp = p.cnt["ln_mm"]
                        p.add("scalar",
                              lambda e, o=selT, i=ps0bf[0:8, 0:512]:
                              e.activation(o, i, AF.Identity),
                              waits=[("ln_mm", seltp)],
                              inc=("ln_act", 1), tag=f"{phase}selte.{c}")
                        K[("selte", c)] = ("ln_act", p.cnt["ln_act"])
                    for half in range(2):
                        for fh in range(2):
                            pi = pass_idx + half * 2 + fh
                            b0 = 2 if pi % 2 == 0 else 4
                            nts = (0, 1) if half == 0 else (2, 3)
                            w = []
                            if half == 0 and fh == 0:
                                w.append(("gelu",
                                          gelu_done[ci * NHB + NHB - 1]))
                                if ci == 0:
                                    w.append(("ln_act",
                                              ln_info["evac_done"][NNT - 1]))
                            if pi >= 2:
                                pp = pi - 2
                                ph_, pf_ = (pp % 4) // 2, pp % 2
                                pci = pp // 4
                                pnts = (0, 1) if ph_ == 0 else (2, 3)
                                w.append(("evac",
                                          evac_done[(pci // NCH, pci % NCH,
                                                     pnts[1], pf_)]))
                            lastpass = lastchunk and half == 1 and fh == 1
                            for hb in range(NHB):
                                ww = list(w) if hb == 0 else []
                                if hb % 4 == 0:
                                    ww.append(K[("w2", call, e_, hb // 4)])
                                for j, nt in enumerate(nts):
                                    p.add("tensor",
                                          lambda e, o=pt[b0 + j][:],
                                          l=hT[:, hb * CH + nt * 128:
                                               hb * CH + (nt + 1) * 128],
                                          r=w2sb[:, hb * 1024 + fh * 512:
                                                 hb * 1024 + fh * 512 + 512],
                                          st=(hb == 0),
                                          sp=(hb == NHB - 1):
                                          e.matmul(o, l, r, start=st,
                                                   stop=sp),
                                          waits=ww if j == 0 else [],
                                          inc=("s2", 1)
                                          if hb == NHB - 1 else
                                          (("w2f", 1)
                                           if (lastpass and j == 1
                                               and hb < NHB - 1) else None),
                                          tag=f"{phase}s2.{e_}.{c}.{pi}.{hb}.{nt}")
                                    if hb == NHB - 1:
                                        s2_done[(e_, c, nt, fh)] = \
                                            p.cnt["s2"]
                            if e_ == E - 1:
                                for j, nt in enumerate(nts):
                                    w = [K[("selte", c)],
                                         ("gelu",
                                          gelu_done[ci * NHB + NHB - 1]),
                                         K["b2r"]]
                                    if ("correv", c, nts[0], fh) in K or pi >= 2:
                                        pass
                                    prevc = K.get(("correv_prev", j))
                                    if prevc is not None:
                                        w.append(prevc)
                                    p.add("tensor",
                                          lambda e, o=pt[6 + j][:],
                                          l=selT[0:8, nt * 128:(nt + 1) * 128],
                                          r=b2r[0:8, fh * 512:fh * 512 + 512]:
                                          e.matmul(o, l, r, start=True,
                                                   stop=True),
                                          waits=w, inc=("corr", 1),
                                          tag=f"{phase}corr.{e_}.{c}.{nt}.{fh}")
                                    K[("corrmm", c, nt, fh)] = \
                                        ("corr", p.cnt["corr"])
                            for j, nt in enumerate(nts):
                                selcol = sel_sb[:, (c * NT_CH + nt) * E + e_:
                                                (c * NT_CH + nt) * E + e_ + 1]
                                wv = [("s2", s2_done[(e_, c, nt, fh)]),
                                      xin_dma[(e_, c, nt)]]
                                prev = prev_nt(e_, c, nt)
                                if prev is not None and (prev[0], prev[1],
                                                         prev[2]) in xout_dma:
                                    wv.append(xout_dma[(prev[0], prev[1],
                                                        prev[2])])
                                p.add("vector",
                                      lambda e,
                                      o=xoutb[nt % 2][:, fh * 512:
                                                      fh * 512 + 512],
                                      i0=pt[b0 + j][:], sc=selcol,
                                      i1=xinb[nt % 2][:, fh * 512:
                                                      fh * 512 + 512]:
                                      e.scalar_tensor_tensor(o, i0, sc, i1,
                                                             ALU.mult,
                                                             ALU.add),
                                      waits=wv,
                                      inc=("ev1", 1) if e_ == E - 1 else
                                      ("evac", 1),
                                      tag=f"{phase}ev.{e_}.{c}.{nt}.{fh}")
                                if e_ == E - 1:
                                    ev1c = p.cnt["ev1"]
                                    p.add("vector",
                                          lambda e,
                                          o=xoutb[nt % 2][:, fh * 512:
                                                          fh * 512 + 512],
                                          b=pt[6 + j][:]:
                                          e.tensor_tensor(o, o, b, ALU.add),
                                          waits=[K[("corrmm", c, nt, fh)],
                                                 ("ev1", ev1c)],
                                          inc=("evac", 1),
                                          tag=f"{phase}ev2.{e_}.{c}.{nt}.{fh}")
                                    K[("correv_prev", j)] = \
                                        ("evac", p.cnt["evac"])
                                evac_done[(e_, c, nt, fh)] = p.cnt["evac"]
                        if half == 0:
                            for nt in nts:
                                g = c * NT_CH + nt
                                dsth = (xfin_hbm if e_ == E - 1 else
                                        xdst_hbm)[g * 128:(g + 1) * 128, :]
                                xo = f"d_xout{nt % 2}"
                                p.add("sync",
                                      lambda e, d=dsth, s=xoutb[nt % 2]:
                                      e.dma_start(out=d, in_=s),
                                      waits=[("evac",
                                              evac_done[(e_, c, nt, 1)])],
                                      inc=(xo, 16),
                                      tag=f"{phase}xo{e_}.{c}.{nt}")
                                xout_dma[(e_, c, nt)] = (xo, p.cnt[xo])
                                emit_xin(nt + 2)
                    if e_ == E - 1:
                        K[("corr_done", c)] = ("evac",
                                               evac_done[(e_, c, 3, 1)])

                    for nt in (2, 3):
                        g = c * NT_CH + nt
                        dst = (xfin_hbm if e_ == E - 1 else xdst_hbm)[
                            g * 128:(g + 1) * 128, :]
                        xo = f"d_xout{nt % 2}"
                        p.add("sync",
                              lambda e, d=dst, s=xoutb[nt % 2]:
                              e.dma_start(out=d, in_=s),
                              waits=[("evac", evac_done[(e_, c, nt, 1)])],
                              inc=(xo, 16),
                              tag=f"{phase}xo{e_}.{c}.{nt}")
                        xout_dma[(e_, c, nt)] = (xo, p.cnt[xo])

                    if lastchunk and e_ < E - 1:
                        for q in range(8):
                            load_w1q(call, e_ + 1, q,
                                     [("s1", s1_done[ci * NHB + 4 * q + 3])],
                                     f"{phase}w1.{e_ + 1}.{q}")
                        for q in range(8):
                            if q < 7:
                                ww2 = [("w2f",
                                        w2f_base + e_ * 31 + 4 * q + 4)]
                            else:
                                ww2 = [("s2", s2_done[(e_, c, NT_CH - 1, 1)])]
                            load_w2q(call, e_ + 1, q, ww2,
                                     f"{phase}w2.{e_ + 1}.{q}")


            return dict(s1_done=s1_done, gelu_done=gelu_done, s2_done=s2_done,
                        evac_done=evac_done, xout_dma=xout_dma)

        def prev_nt(e_, c, nt):
            if nt >= 2:
                return (e_, c, nt - 2)
            if c > 0:
                return (e_, c - 1, nt + NT_CH - 2)
            if e_ > 0:
                return (e_ - 1, NCH - 1, nt + NT_CH - 2)
            return None

        def emit_attn(ffn1, xwait_fn, phase="A."):
            mm, act, dv = {}, {}, {}
            s2f = ffn1["s2_done"][(E - 1, NCH - 1, NT_CH - 1, 1)]
            evf = ffn1["evac_done"][(E - 1, NCH - 1, NT_CH - 1, 1)]
            glf = ffn1["gelu_done"][E * NCH * NHB - 1]
            dma_finals = [(s, p.cnt[s]) for s in
                          ("d_xout0", "d_xout1", "d_xin0", "d_xin1",
                           "d_utb0", "d_utb1")]

            p.add("gpsimd", lambda e: e.dma_start(
                out=at_wqkv.rearrange("p (dt j) -> p dt j", dt=NDT),
                in_=wqkv_in.rearrange("(dt p) j -> p dt j", p=128)),
                waits=[("s2", s2f)], inc=("d_wqkv", 16), tag=phase + "wqkv")
            K["wqkv"] = ("d_wqkv", p.cnt["d_wqkv"])
            p.add("gpsimd", lambda e: e.dma_start(
                out=at_wout.rearrange("p (jt f) -> p jt f", jt=NDT),
                in_=wout_in.rearrange("(jt p) f -> p jt f", p=128)),
                waits=[("s2", s2f)], inc=("d_wout", 16), tag=phase + "wout")
            K["wout"] = ("d_wout", p.cnt["d_wout"])

            def pass1_chunk(c, ln):
                lnrdy = ln["evac_done"][c * NT_CH + NT_CH - 1]
                for jt in range(NDT):
                    w = []
                    if jt == 0:
                        w += [("ln_act", lnrdy), K["wqkv"]]
                        if c == 0:
                            w += [("gelu", glf), ("evac", evf)] + dma_finals
                    kk = ("kt", c, jt)
                    prev = ("kt", c, jt - 2) if jt >= 2 else (
                        ("v", c - 1, 6 + jt) if c > 0 else None)
                    if prev is not None and prev in act:
                        w.append(("at_act", act[prev]))
                    bank = pt[jt % 2]
                    for dt in range(NDT):
                        p.add("tensor",
                              lambda e, o=bank[:],
                              l=at_wqkv[:, dt * 3072 + 1024 + jt * 128:
                                        dt * 3072 + 1024 + (jt + 1) * 128],
                              r=at_uT[:, dt * 4096 + c * CH:
                                      dt * 4096 + (c + 1) * CH],
                              st=(dt == 0), sp=(dt == NDT - 1):
                              e.matmul(o, l, r, start=st, stop=sp),
                              waits=w if dt == 0 else [],
                              inc=("at_mm", 1) if dt == NDT - 1 else None,
                              tag=f"{phase}kt.{c}.{jt}.{dt}")
                    mm[kk] = p.cnt["at_mm"]
                    w2 = [("at_mm", mm[kk])]
                    if c > 0:
                        w2.append(("at_mm", mm[("phik", c - 1, 7)]))
                    p.add("scalar",
                          lambda e, o=at_kT[:, jt * 512:(jt + 1) * 512],
                          i=bank[:]: e.activation(o, i, AF.Identity),
                          waits=w2, inc=("at_act", 1),
                          tag=f"{phase}kte.{c}.{jt}")
                    act[kk] = p.cnt["at_act"]

                for nt in range(NT_CH):
                    for hf in range(2):
                        vi = nt * 2 + hf
                        w = []
                        prev = ("v", c, vi - 2) if vi >= 2 else \
                            ("kt", c, NDT - 2 + vi)
                        if prev in act:
                            w.append(("at_act", act[prev]))
                        bank = pt[vi % 2]
                        for dt in range(NDT):
                            p.add("tensor",
                                  lambda e, o=bank[:],
                                  l=at_uT[:, dt * 4096 + (c * NT_CH + nt) * 128:
                                          dt * 4096 + (c * NT_CH + nt + 1) * 128],
                                  r=at_wqkv[:, dt * 3072 + 2048 + hf * 512:
                                            dt * 3072 + 2048 + hf * 512 + 512],
                                  st=(dt == 0), sp=(dt == NDT - 1):
                                  e.matmul(o, l, r, start=st, stop=sp),
                                  waits=w if dt == 0 else [],
                                  inc=("at_mm", 1) if dt == NDT - 1 else None,
                                  tag=f"{phase}v.{c}.{vi}.{dt}")
                        mm[("v", c, vi)] = p.cnt["at_mm"]
                        w2 = [("at_mm", mm[("v", c, vi)])]
                        if c > 0:
                            w2.append(("at_mm", mm[("kv", c - 1, PAIRS - 1)]))
                        p.add("scalar",
                              lambda e,
                              o=at_v[:, nt * 1024 + hf * 512:
                                     nt * 1024 + hf * 512 + 512],
                              i=bank[:]: e.activation(o, i, AF.Identity),
                              waits=w2, inc=("at_act", 1),
                              tag=f"{phase}ve.{c}.{vi}")
                        act[("v", c, vi)] = p.cnt["at_act"]

                for nt in range(NT_CH):
                    for half in range(2):
                        bank = pt[2 + half]
                        w = []
                        prev = ("phik", c, (nt - 1) * 2 + half) if nt >= 1 \
                            else (("phik", c - 1, 6 + half) if c > 0 else None)
                        if prev is not None and prev in act:
                            w.append(("at_act", act[prev]))
                        if c == 0 and nt == 0:
                            w.append(K["rf"])
                        for pr4 in range(4):
                            pr = half * 4 + pr4
                            p.add("tensor",
                                  lambda e, o=bank[:, pr4 * 128:(pr4 + 1) * 128],
                                  l=at_kT[:, pr * 512 + nt * 128:
                                          pr * 512 + (nt + 1) * 128],
                                  r=rf_sb[:, pr * 128:(pr + 1) * 128]:
                                  e.matmul(o, l, r, start=True, stop=True),
                                  waits=(w + [("at_act", act[("kt", c, pr)])])
                                  if pr4 == 0 else
                                  [("at_act", act[("kt", c, pr)])],
                                  inc=("at_mm", 1) if pr4 == 3 else None,
                                  tag=f"{phase}pk.{c}.{nt}.{pr}")
                        mm[("phik", c, nt * 2 + half)] = p.cnt["at_mm"]
                        wpe = [("at_mm", mm[("phik", c, nt * 2 + half)])]
                        if c > 0 and nt == 0:
                            wpe.append(("at_mm", mm[("kv", c - 1, PAIRS - 1)]))
                        p.add("scalar",
                              lambda e,
                              o=at_phik[:, nt * 1024 + half * 512:
                                        nt * 1024 + half * 512 + 512],
                              i=bank[:]: e.activation(o, i, AF.Relu),
                              waits=wpe,
                              inc=("at_act", 1),
                              tag=f"{phase}pke.{c}.{nt}.{half}")
                        act[("phik", c, nt * 2 + half)] = p.cnt["at_act"]

                for pr in range(PAIRS):
                    bank = pt[6 + pr % 2]
                    for nt in range(NT_CH):
                        w = [("at_act", act[("phik", c, nt * 2 + pr // 4)]),
                             ("at_act", act[("v", c, nt * 2 + 0)]),
                             ("at_act", act[("v", c, nt * 2 + 1)])]
                        if nt == 0:
                            bprev = (c, pr - 2) if pr >= 2 else (c - 1, 6 + pr)
                            if ("kvadd", bprev[0], bprev[1]) in dv:
                                w.append(("at_dve",
                                          dv[("kvadd", bprev[0], bprev[1])]))
                            if c == 0 and pr < 2:
                                w.append(("evac", evf))
                                w.append(("gate_dve", K[("sel", "L1.")]))
                        p.add("tensor",
                              lambda e, o=bank[:, 0:128],
                              l=at_phik[:, nt * 1024 + pr * 128:
                                        nt * 1024 + (pr + 1) * 128],
                              r=at_v[:, nt * 1024 + pr * 128:
                                     nt * 1024 + (pr + 1) * 128],
                              st=(nt == 0):
                              e.matmul(o, l, r, start=st, stop=False),
                              waits=w, tag=f"{phase}kv.{c}.{pr}.{nt}")
                        wks = []
                        if c == 0 and pr == 0 and nt == 0:
                            wks = [("gp_c", K["ones"])]
                        p.add("tensor",
                              lambda e, o=bank[:, 128:129],
                              l=at_phik[:, nt * 1024 + pr * 128:
                                        nt * 1024 + (pr + 1) * 128],
                              sp=(nt == NT_CH - 1):
                              e.matmul(o, l, ones_col, start=False, stop=sp),
                              waits=wks,
                              inc=("at_mm", 1) if nt == NT_CH - 1 else None,
                              tag=f"{phase}ks.{c}.{pr}.{nt}")
                    mm[("kv", c, pr)] = p.cnt["at_mm"]
                    dst = kv_acc[:, pr * 132:pr * 132 + 129]
                    if c == 0:
                        p.add("vector",
                              lambda e, o=dst, i=bank[:, 0:129]:
                              e.tensor_copy(o, i),
                              waits=[("at_mm", mm[("kv", c, pr)])],
                              inc=("at_dve", 1), tag=f"{phase}kva.{c}.{pr}")
                    else:
                        p.add("vector",
                              lambda e, o=dst, a=dst, b=bank[:, 0:129]:
                              e.tensor_tensor(o, a, b, ALU.add),
                              waits=[("at_mm", mm[("kv", c, pr)])],
                              inc=("at_dve", 1), tag=f"{phase}kva.{c}.{pr}")
                    dv[("kvadd", c, pr)] = p.cnt["at_dve"]

            ln2 = emit_ln(xbuf, gate=False, ut_dst="sbuf", xwait_fn=xwait_fn,
                          phase="L2.", chunk_cb=pass1_chunk)
            ln2f = ln2["evac_done"][NNT - 1]

            kvf = p.cnt["at_dve"]
            kvw = [("at_dve", dv[("kvadd", NCH - 1, pq)]) for pq in range(PAIRS)]
            kvmmf = mm[("kv", NCH - 1, PAIRS - 1)]
            for pr in range(PAIRS):
                for q in range(2):
                    p.add("scalar",
                          lambda e,
                          o=kv_bd[64 * q:64 * (q + 1),
                                  pr * 128 + 64 * q:pr * 128 + 64 * (q + 1)],
                          i=kv_acc[64 * q:64 * (q + 1),
                                   pr * 132 + 64 * q:pr * 132 + 64 * (q + 1)]:
                          e.activation(o, i, AF.Identity),
                          waits=(kvw + [("gp_c", K["ones"])])
                          if (pr, q) == (0, 0) else [],
                          inc=("at_act", 1), tag=f"{phase}kvbd.{pr}.{q}")
            act["kvbd"] = p.cnt["at_act"]
            p.add("vector",
                  lambda e: e.tensor_tensor(
                      ksum_bd.rearrange("p (a q) -> p a q", q=2),
                      kv_acc.rearrange("p (a k) -> p a k", k=132)[:, :, 128:129]
                      .broadcast_to([128, 8, 2]),
                      ksum_qm.rearrange("p (a q) -> p a q", q=2),
                      ALU.mult),
                  waits=[K["qmask"]] + kvw,
                  inc=("at_dve", 1), tag=phase + "ksbd")
            dv["ksbd"] = p.cnt["at_dve"]

            for c in range(NCH):
                for jt in range(NDT):
                    w = []
                    if c == 0 and jt < 2:
                        w.append(("at_act", act[("v", NCH - 1, 6 + jt)]))
                    if c > 0 and jt < 2:
                        if jt == 0:
                            w.append(("at_dve", dv[("attnT", c - 1, PAIRS - 1)]))
                        else:
                            w.append(("at_act", act[("nes", c - 1, PAIRS - 1)]))
                    prev = ("qt", c, jt - 2) if jt >= 2 else None
                    if prev is not None and prev in act:
                        w.append(("at_act", act[prev]))
                    bank = pt[jt % 2]
                    for dt in range(NDT):
                        p.add("tensor",
                              lambda e, o=bank[:],
                              l=at_wqkv[:, dt * 3072 + jt * 128:
                                        dt * 3072 + (jt + 1) * 128],
                              r=at_uT[:, dt * 4096 + c * CH:
                                      dt * 4096 + (c + 1) * CH],
                              st=(dt == 0), sp=(dt == NDT - 1):
                              e.matmul(o, l, r, start=st, stop=sp),
                              waits=w if dt == 0 else [],
                              inc=("at_mm", 1) if dt == NDT - 1 else None,
                              tag=f"{phase}qt.{c}.{jt}.{dt}")
                    mm[("qt", c, jt)] = p.cnt["at_mm"]
                    wq = [("at_mm", mm[("qt", c, jt)])]
                    if c == 0:
                        wq.append(("at_mm", kvmmf))
                    p.add("scalar",
                          lambda e, o=at_qT[:, jt * 512:(jt + 1) * 512],
                          i=bank[:]: e.activation(o, i, AF.Identity),
                          waits=wq, inc=("at_act", 1),
                          tag=f"{phase}qte.{c}.{jt}")
                    act[("qt", c, jt)] = p.cnt["at_act"]

                for pr in range(PAIRS):
                    bank = pt[2 + pr % 2]
                    w = [("at_act", act[("qt", c, pr)])]
                    prev = ("phiq", c, pr - 2) if pr >= 2 else (
                        ("phiq", c - 1, PAIRS - 2 + pr) if c > 0 else None)
                    if prev is not None and prev in act:
                        w.append(("at_act", act[prev]))
                    if c == 0 and pr < 2:
                        w.append(("at_act", act[("phik", NCH - 1, 6 + pr)]))
                    if c > 0 and pr < 2:
                        w.append(("at_dve", dv[("pqs", c - 1, 6 + pr)]))
                    p.add("tensor",
                          lambda e, o=bank[:],
                          l=rf_sb[:, pr * 128:(pr + 1) * 128],
                          r=at_qT[:, pr * 512:(pr + 1) * 512]:
                          e.matmul(o, l, r, start=True, stop=True),
                          waits=w, inc=("at_mm", 1),
                          tag=f"{phase}pq.{c}.{pr}")
                    mm[("phiq", c, pr)] = p.cnt["at_mm"]
                    wpq = [("at_mm", mm[("phiq", c, pr)])]
                    if c == 0:
                        wpq.append(("at_mm", kvmmf))
                    p.add("scalar",
                          lambda e, o=at_phiq[:, pr * 512:(pr + 1) * 512],
                          i=bank[:]: e.activation(o, i, AF.Relu),
                          waits=wpq,
                          inc=("at_act", 1), tag=f"{phase}pqe.{c}.{pr}")
                    act[("phiq", c, pr)] = p.cnt["at_act"]

                def emit_nrm(pr):
                    q = pr % 2
                    nrmbank = pt[5] if q == 0 else pt[1]
                    wn = []
                    if (c, pr) == (0, 0):
                        wn.append(("at_act", act["kvbd"]))
                        wn.append(("at_dve", dv["ksbd"]))
                    wn.append(("at_act", act[("phiq", c, pr)]))
                    prevn = ("nes", c, pr - 2) if pr >= 2 else (
                        ("nes", c - 1, PAIRS - 2 + pr) if c > 0 else None)
                    if prevn is not None and prevn in act and \
                            act[prevn] is not None:
                        wn.append(("at_act", act[prevn]))
                    if pr == 1:
                        wn.append(("at_act", act[("qt", c, NDT - 1)]))
                    p.add("tensor",
                          lambda e, o=nrmbank[0:2, :],
                          l=ksum_bd[:, pr * 2:(pr + 1) * 2],
                          r=at_phiq[:, pr * 512:(pr + 1) * 512]:
                          e.matmul(o, l, r, start=True, stop=True),
                          waits=wn, inc=("at_mm", 1), tag=f"{phase}n.{c}.{pr}")
                    mm[("nrm", c, pr)] = p.cnt["at_mm"]

                def emit_chain(pr):
                    q = pr % 2
                    nrmbank = pt[5] if q == 0 else pt[1]
                    rbmbank = pt[2 + q]
                    wne = [("at_mm", mm[("nrm", c, pr)])]
                    if ("rc", c, pr - 2) in act and \
                            act[("rc", c, pr - 2)] is not None:
                        wne.append(("at_act", act[("rc", c, pr - 2)]))
                    elif c > 0 and ("rc", c - 1, PAIRS - 2 + pr) in act:
                        wne.append(("at_act", act[("rc", c - 1,
                                                   PAIRS - 2 + pr)]))
                    p.add("scalar",
                          lambda e, o=nrm_sb[q], i=nrmbank[0:2, :]:
                          e.activation(o, i, AF.Ln, bias=np.float32(STAB)),
                          waits=wne, inc=("at_act", 1),
                          tag=f"{phase}ln.{c}.{pr}")
                    lnc = p.cnt["at_act"]
                    p.add("scalar",
                          lambda e, o=rec_sb[q], i=nrm_sb[q]:
                          e.activation(o, i, AF.Exp, scale=-1.0),
                          waits=[("at_act", lnc)], inc=("at_act", 1),
                          tag=f"{phase}ex.{c}.{pr}")
                    exc = p.cnt["at_act"]
                    dv[("nes", c, pr)] = None
                    act[("nes", c, pr)] = lnc
                    c1 = exc
                    dv[("rc", c, pr)] = None
                    act[("rc", c, pr)] = exc
                    wc2 = [("at_act", exc)]
                    if ("rbm", c, pr - 2) in mm:
                        wc2.append(("at_mm", mm[("rbm", c, pr - 2)]))
                    elif c > 0:
                        wc2.append(("at_mm", mm[("rbm", c - 1,
                                                 PAIRS - 2 + pr)]))
                    c2 = p.add("vector",
                               lambda e, o=rec_bf[q], i=rec_sb[q]:
                               e.tensor_copy(o, i),
                               waits=wc2, inc=("at_dve", 1),
                               tag=f"{phase}rb.{c}.{pr}")

                    wrb = [("at_dve", c2), K["bcmask"]]
                    if pr < 2:
                        wrb.append(("at_act", act[("phiq", c, 6 + pr)]))
                    else:
                        wrb.append(("at_dve", dv[("pqs", c, pr - 2)]))
                    p.add("tensor",
                          lambda e, o=rbmbank[:], rb=rec_bf[q]:
                          e.matmul(o, bcmask, rb, start=True, stop=True),
                          waits=wrb, inc=("at_mm", 1),
                          tag=f"{phase}rbm.{c}.{pr}")
                    mm[("rbm", c, pr)] = p.cnt["at_mm"]

                    p.add("vector",
                          lambda e, o=at_phiq[:, pr * 512:(pr + 1) * 512],
                          b=rbmbank[:]:
                          e.tensor_tensor(o, o, b, ALU.mult),
                          waits=[("at_mm", mm[("rbm", c, pr)])],
                          inc=("at_dve", 1), tag=f"{phase}pqs.{c}.{pr}")
                    dv[("pqs", c, pr)] = p.cnt["at_dve"]

                for pr in range(PAIRS + 1):
                    if pr < PAIRS:
                        emit_nrm(pr)
                    if pr >= 1:
                        emit_chain(pr - 1)

                for pr in range(PAIRS):
                    q = pr % 2
                    outbank = pt[4] if q == 0 else pt[0]
                    w = [("at_dve", dv[("pqs", c, pr)])]
                    if (c, pr) == (0, 0):
                        w.append(("at_act", act["kvbd"]))
                    prev = ("attnT", c, pr - 2) if pr >= 2 else (
                        ("attnT", c - 1, PAIRS - 2 + pr) if c > 0 else None)
                    if prev is not None and prev in dv:
                        w.append(("at_dve", dv[prev]))
                    if pr == 1:
                        w.append(("at_act", act[("qt", c, NDT - 2)]))
                    p.add("tensor",
                          lambda e, o=outbank[:],
                          l=kv_bd[:, pr * 128:(pr + 1) * 128],
                          r=at_phiq[:, pr * 512:(pr + 1) * 512]:
                          e.matmul(o, l, r, start=True, stop=True),
                          waits=w, inc=("at_mm", 1), tag=f"{phase}o.{c}.{pr}")
                    mm[("o", c, pr)] = p.cnt["at_mm"]
                    wat = [("at_mm", mm[("o", c, pr)])]
                    if c == 0 and pr < 2:
                        wat.append(("at_mm", kvmmf))
                    p.add("vector",
                          lambda e, o=at_attnT[:, pr * 512:(pr + 1) * 512],
                          a=outbank[:]:
                          e.tensor_copy(o, a),
                          waits=wat,
                          inc=("at_dve", 1), tag=f"{phase}at.{c}.{pr}")
                    dv[("attnT", c, pr)] = p.cnt["at_dve"]

                def emit_at_xin(nt):
                    w = [ffn1["xout_dma"][(E - 1, c, nt)]]
                    prev = prev_at_nt(c, nt)
                    if prev is not None:
                        w.append(("at_dve", dv[("wev", prev[0], prev[1], 1)]))
                    if c == 0 and nt < 2:
                        w.append(("evac", evf))
                        w.append(("ln_act", ln2f))
                    g = c * NT_CH + nt
                    asem = f"d_atxi{nt % 2}"
                    p.add("sync",
                          lambda e, d=at_xin[nt % 2],
                          s=xbuf[g * 128:(g + 1) * 128, :]:
                          e.dma_start(out=d, in_=s),
                          waits=w, inc=(asem, 16),
                          tag=f"{phase}xin.{c}.{nt}")
                    K[("at_xin", c, nt)] = (asem, p.cnt[asem])

                for nt in range(2):
                    emit_at_xin(nt)

                for nt in range(NT_CH):
                    for fh in range(2):
                        gi = nt * 2 + fh
                        bank = pt[6 + gi % 2]
                        w = [("at_dve", dv[("attnT", c, PAIRS - 1)]),
                             K["wout"]]
                        if c == 0 and gi < 2:
                            w.append(("at_act", act["kvbd"]))
                        prev = ("wo", c, gi - 2) if gi >= 2 else (
                            ("wo", c - 1, 6 + gi) if c > 0 else None)
                        if prev is not None:
                            pnt, pfh = prev[2] // 2, prev[2] % 2
                            w.append(("at_dve", dv[("wev", prev[1], pnt, pfh)]))
                        for jt in range(NDT):
                            p.add("tensor",
                                  lambda e, o=bank[:],
                                  l=at_attnT[:, jt * 512 + nt * 128:
                                             jt * 512 + (nt + 1) * 128],
                                  r=at_wout[:, jt * 1024 + fh * 512:
                                            jt * 1024 + fh * 512 + 512],
                                  st=(jt == 0):
                                  e.matmul(o, l, r, start=st, stop=False),
                                  waits=w if jt == 0 else [],
                                  tag=f"{phase}wo.{c}.{gi}.{jt}")
                        p.add("tensor",
                              lambda e, o=bank[:],
                              r=bout_row[0:1, fh * 512:fh * 512 + 512]:
                              e.matmul(o, ones_row, r, start=False, stop=True),
                              waits=[K["bout"]],
                              inc=("at_mm", 1), tag=f"{phase}wob.{c}.{gi}")
                        mm[("wo", c, gi)] = p.cnt["at_mm"]

                        wv = [("at_mm", mm[("wo", c, gi)]),
                              K[("at_xin", c, nt)]]
                        prevx = prev_at_nt(c, nt)
                        if prevx is not None:
                            wv.append(K[("at_xout", prevx[0], prevx[1])])
                        p.add("vector",
                              lambda e,
                              o=at_xout[nt % 2][:, fh * 512:fh * 512 + 512],
                              a=bank[:],
                              b=at_xin[nt % 2][:, fh * 512:fh * 512 + 512]:
                              e.tensor_tensor(o, a, b, ALU.add),
                              waits=wv, inc=("at_dve", 1),
                              tag=f"{phase}wev.{c}.{nt}.{fh}")
                        dv[("wev", c, nt, fh)] = p.cnt["at_dve"]

                    g = c * NT_CH + nt
                    osem = f"d_atxo{nt % 2}"
                    p.add("sync",
                          lambda e, d=xbuf[g * 128:(g + 1) * 128, :],
                          s=at_xout[nt % 2]: e.dma_start(out=d, in_=s),
                          waits=[("at_dve", dv[("wev", c, nt, 1)])],
                          inc=(osem, 16), tag=f"{phase}xo.{c}.{nt}")
                    K[("at_xout", c, nt)] = (osem, p.cnt[osem])
                    if nt + 2 < NT_CH:
                        emit_at_xin(nt + 2)

            return dict(mm=mm, act=act, dv=dv, ln2=ln2)

        def prev_at_nt(c, nt):
            if nt >= 2:
                return (c, nt - 2)
            if c > 0:
                return (c - 1, nt + NT_CH - 2)
            return None

        ln1 = emit_ln(x_in, gate=True, ut_dst="hbm", phase="L1.")

        if upto == "p1":
            if dbg_sel is not None:
                p.add("sync", lambda e: e.dma_start(out=dbg_sel, in_=sel_sb),
                      waits=[("gate_dve", K[("sel", "L1.")])],
                      inc=("d_cst_s", 16), tag="dbgsel")
            final_wait(p)
            p.verify()
            p.emit(nc, ctx)
            return nc

        ffn1 = emit_ffn(0, x_in, xbuf, xbuf, "L1.", ln1, phase="F1.")

        if upto == "ffn1":
            if dbg_sel is not None:
                p.add("sync", lambda e: e.dma_start(out=dbg_sel, in_=sel_sb),
                      waits=[("gate_dve", K[("sel", "L1.")])],
                      inc=("d_cst_s", 16), tag="dbgsel")
            final_wait(p)
            p.verify()
            p.emit(nc, ctx)
            return nc

        s2f = ffn1["s2_done"][(E - 1, NCH - 1, NT_CH - 1, 1)]
        evf = ffn1["evac_done"][(E - 1, NCH - 1, NT_CH - 1, 1)]

        def ln2_xwait(c, nt):
            return [ffn1["xout_dma"][(E - 1, c, nt)],
                    ("s2", s2f), ("evac", evf)]

        attn = emit_attn(ffn1, ln2_xwait, phase="A.")

        if upto == "attn":
            final_wait(p)
            p.verify()
            p.emit(nc, ctx)
            return nc

        at_mm_f = p.cnt["at_mm"]
        at_act_f = p.cnt["at_act"]
        at_dve_f = p.cnt["at_dve"]
        atxo_f = [("d_atxo0", p.cnt["d_atxo0"]), ("d_atxo1", p.cnt["d_atxo1"]),
                  ("d_atxi0", p.cnt["d_atxi0"]), ("d_atxi1", p.cnt["d_atxi1"])]

        def ln3_xwait(c, nt):
            return [K[("at_xout", c, nt)],
                    ("at_mm", at_mm_f), ("at_dve", at_dve_f),
                    ("at_act", at_act_f)] + atxo_f

        ln3 = emit_ln(xbuf, gate=True, ut_dst="hbm", xwait_fn=ln3_xwait,
                      phase="L3.")

        def ffn2_xsrc_wait(c, nt):
            return [K[("at_xout", c, nt)],
                    ("at_mm", at_mm_f), ("at_dve", at_dve_f),
                    ("at_act", at_act_f)] + atxo_f

        ffn2 = emit_ffn(1, xbuf, xbuf, y, "L3.", ln3,
                        wregion_waits=[("at_mm", at_mm_f)],
                        xtra_first_waits=[("at_dve", at_dve_f),
                                          ("at_act", at_act_f)],
                        xsrc_wait=ffn2_xsrc_wait, phase="F2.")

        final_wait(p)
        p.verify()
        p.emit(nc, ctx)
    return nc


def final_wait(p):
    waits = [(s, v) for s, v in p.cnt.items() if s.startswith("d_") and v > 0]
    p.add("sync", lambda e: None, waits=waits, tag="final")


def kernel(L0_attn_x, L0_attn_W_qkv, L0_attn_RF, L0_attn_W_out, L0_attn_b_out,
           L0_ffn_Wg, L0_ffn_bg, L0_ffn_W1, L0_ffn_b1, L0_ffn_W2, L0_ffn_b2):
    nc = build_program(debug_taps=DEBUG_TAPS, upto=UPTO)

    b1t = np.ascontiguousarray(
        np.asarray(L0_ffn_b1, dtype=np.float32).reshape(E, NHB, 128)
        .transpose(0, 2, 1))
    ident = np.eye(128, dtype=ml_dtypes.bfloat16)
    bcm = np.zeros((2, 128), dtype=ml_dtypes.bfloat16)
    bcm[0, 0:64] = 1
    bcm[1, 64:128] = 1
    qm = np.zeros((128, 16), dtype=ml_dtypes.bfloat16)
    qm.reshape(128, 8, 2)[0:64, :, 0] = 1
    qm.reshape(128, 8, 2)[64:128, :, 1] = 1

    common = dict(
        wqkv=np.asarray(L0_attn_W_qkv, dtype=np.float32),
        rf=np.asarray(L0_attn_RF, dtype=np.float32),
        wout=np.asarray(L0_attn_W_out, dtype=np.float32),
        bout=np.asarray(L0_attn_b_out, dtype=np.float32).reshape(1, D),
        wg=np.asarray(L0_ffn_Wg, dtype=np.float32),
        bg=np.asarray(L0_ffn_bg, dtype=np.float32).reshape(1, E),
        w1=np.asarray(L0_ffn_W1, dtype=np.float32),
        b1t=b1t,
        w2=np.asarray(L0_ffn_W2, dtype=np.float32),
        b2=np.asarray(L0_ffn_b2, dtype=np.float32),
        ident=ident,
        bcmask=bcm,
        qmask=qm,
    )
    xs = np.asarray(L0_attn_x, dtype=np.float32)
    in_maps = [dict(common, x=np.ascontiguousarray(xs[i])) for i in range(8)]
    res = run_bass_kernel_spmd(nc, in_maps, core_ids=list(range(8)))
    return np.stack([np.asarray(r["y"]) for r in res.results], axis=0)
